# version 25
# baseline (speedup 1.0000x reference)
import math
from contextlib import ExitStack
import numpy as np
import concourse.bass as bass
import concourse.mybir as mybir
from concourse.bass_utils import run_bass_kernel_spmd

F32 = mybir.dt.float32
BF16 = mybir.dt.bfloat16
U16 = mybir.dt.uint16
AF = mybir.ActivationFunctionType
ALU = mybir.AluOpType

NCORES = 8
S = 16384
D = 1024
HD = 128
DH = 4096
TLOC = S // NCORES
P1CH = 512
NP1 = TLOC // P1CH
QBLK = 512
NQB = TLOC // QBLK
FCH = 1024
NFCH = TLOC // FCH
NPAN = S // 128
EPS = 1e-5
ISQ = 1.0 / math.sqrt(HD)
CSHIFT = 18.0
MAGIC = float(np.float32(1.5 * 2 ** 23))
INV2PI = float(np.float32(1.0 / (2 * math.pi)))
_C1 = np.float32(6.28125)
_C2 = np.float32(2 * math.pi - float(_C1))
_C3 = np.float32(2 * math.pi - float(_C1) - float(_C2))
C1, C2, C3 = float(_C1), float(_C2), float(_C3)
LN1e4_64 = math.log(10000.0) / 64.0
PI_LO = float(np.nextafter(np.float32(np.pi), np.float32(0.0)))

ENGINES = ("tensor", "vector", "scalar", "gpsimd", "sync")


class Sem:
    def __init__(self, handle, name):
        self.h = handle
        self.name = name
        self.n = 0

    def bump(self, k=1):
        self.n += k
        return self.n


class Plan:
    def __init__(self):
        self.steps = {e: [] for e in ENGINES}
        self.hwm = {e: {} for e in ENGINES}
        self.self_sems = {}

    def step(self, eng, fn, waits=(), inc=None, inc_k=1, force_waits=False):
        ss = self.self_sems.get(eng)
        if ss is not None and ss.n > 0:
            waits = tuple(waits) + ((ss, ss.n),)
        w = []
        for sem, val in waits:
            if sem is None or val is None or val <= 0:
                continue
            if not force_waits and self.hwm[eng].get(sem.name, 0) >= val:
                continue
            self.hwm[eng][sem.name] = max(self.hwm[eng].get(sem.name, 0), val)
            w.append((sem, val))
        self.steps[eng].append((w, fn, inc, inc_k))
        if inc is not None:
            return inc.bump(inc_k)
        return None

    def barrier(self, sbar, extra_waits=(), engines=ENGINES):
        base = sbar.n
        for e in engines:
            self.step(e, lambda g: g.nop(), inc=sbar)
        for e in engines:
            self.step(e, None,
                      waits=tuple(extra_waits) + ((sbar, base + len(engines)),))

    def emit(self, block):
        def mk(eng_name):
            steps = self.steps[eng_name]

            def body(g):
                for w, fn, inc, inc_k in steps:
                    for sem, val in w:
                        g.wait_ge(sem.h, val)
                    if fn is None:
                        continue
                    ins = fn(g)
                    if inc is not None:
                        assert ins is not None, f"inc step returned None ({eng_name})"
                        ins.then_inc(inc.h, inc_k)
            return body
        block.tensor(mk("tensor"))
        block.vector(mk("vector"))
        block.scalar(mk("scalar"))
        block.gpsimd(mk("gpsimd"))
        block.sync(mk("sync"))


def build_program(taps=()):
    nc = bass.Bass("TRN2", target_bir_lowering=False)

    x_d = nc.dram_tensor("x", [TLOC, D], F32, kind="ExternalInput")
    wqT_d = nc.dram_tensor("wqT", [D, HD], F32, kind="ExternalInput")
    wkT_d = nc.dram_tensor("wkT", [D, HD], F32, kind="ExternalInput")
    wvT_d = nc.dram_tensor("wvT", [D, HD], F32, kind="ExternalInput")
    woT_d = nc.dram_tensor("woT", [HD, D], F32, kind="ExternalInput")
    w1T_d = nc.dram_tensor("w1T", [D, DH], F32, kind="ExternalInput")
    w2T_d = nc.dram_tensor("w2T", [D, DH], F32, kind="ExternalInput")
    h1T_d = nc.dram_tensor("h1T", [DH, DH], F32, kind="ExternalInput")
    h2T_d = nc.dram_tensor("h2T", [DH, DH], F32, kind="ExternalInput")
    w3T_d = nc.dram_tensor("w3T", [DH, D], F32, kind="ExternalInput")
    s1_d = nc.dram_tensor("s1c", [128, DH // 128], F32, kind="ExternalInput")
    s2_d = nc.dram_tensor("s2c", [128, DH // 128], F32, kind="ExternalInput")
    hs1_d = nc.dram_tensor("hs1c", [128, DH // 128], F32, kind="ExternalInput")
    hs2_d = nc.dram_tensor("hs2c", [128, DH // 128], F32, kind="ExternalInput")
    s3_d = nc.dram_tensor("s3c", [128, D // 128], F32, kind="ExternalInput")
    coff_d = nc.dram_tensor("coffm1", [128, 1], F32, kind="ExternalInput")
    out_d = nc.dram_tensor("out", [TLOC, D], F32, kind="ExternalOutput")

    kvs_d = nc.dram_tensor("kvs", [128, 2 * TLOC], BF16)
    kvg_d = nc.dram_tensor("kvg", [NCORES * 128, 2 * TLOC], BF16)
    x1_d = nc.dram_tensor("x1d", [TLOC, D], F32)
    w1b_d = nc.dram_tensor("w1b", [D, DH], BF16)
    w2b_d = nc.dram_tensor("w2b", [D, DH], BF16)
    h1b_d = nc.dram_tensor("h1b", [DH, DH], BF16)
    h2b_d = nc.dram_tensor("h2b", [DH, DH], BF16)
    w3b_d = nc.dram_tensor("w3b", [DH, D], BF16)

    tap_d = {}
    tap_shapes = {"kt": ([128, S], U16), "v": ([128, 16 * 130], U16),
                  "q": ([128, TLOC], U16), "x1": ([TLOC, D], F32),
                  "kloc": ([128, TLOC], U16), "vloc": ([128, 2048], U16),
                  "lnx": ([128, 4096], U16)}
    for t in taps:
        sh, dt = tap_shapes[t]
        tap_d[t] = nc.dram_tensor("tap_" + t, sh, dt, kind="ExternalOutput")

    pl = Plan()
    es_sem = ExitStack()
    dma_sems = []

    def mksem(name, dma=False):
        s = Sem(es_sem.enter_context(nc.semaphore(name)), name)
        if dma:
            dma_sems.append(s)
        return s

    with nc.Block() as block:
        sPE = mksem("s_pe")
        sACT = mksem("s_act")
        sDVE = mksem("s_dve")
        sGPC = mksem("s_gpc")
        pl.self_sems = {"vector": sDVE, "scalar": sACT, "gpsimd": sGPC}
        sCC = mksem("s_cc")
        sBAR = mksem("s_bar")
        sEXP = mksem("s_exp")
        sSPE = mksem("s_spe")
        sAPE = mksem("s_ape")
        sOUT = [mksem("s_out0", dma=True), mksem("s_out1", dma=True)]
        sSET = mksem("s_set", dma=True)
        sXC = [mksem("s_xc0", dma=True), mksem("s_xc1", dma=True)]
        sKSW = mksem("s_ksw", dma=True)
        sKVST = mksem("s_kvst", dma=True)
        sKV = mksem("s_kv", dma=True)
        sKV2 = mksem("s_kv2", dma=True)
        sTAB = mksem("s_tab", dma=True)
        sCVA = mksem("s_cva", dma=True)
        sCVH1 = mksem("s_cvh1", dma=True)
        sCVH2 = mksem("s_cvh2", dma=True)
        sWOV = mksem("s_wov", dma=True)
        sXOR = [mksem(f"s_xor{i}", dma=True) for i in range(8)]
        sSP = [mksem(f"s_sp{i}", dma=True) for i in range(4)]
        sX1 = [mksem("s_x10", dma=True), mksem("s_x11", dma=True)]
        sW12 = [mksem("s_w120", dma=True), mksem("s_w121", dma=True)]
        sMID = [mksem("s_mid0", dma=True), mksem("s_mid1", dma=True)]
        sTAP = mksem("s_tap", dma=True)
        sPACE = mksem("s_pace")

        def dma_snapshot(exclude=()):
            return [(s, s.n) for s in dma_sems if s.name not in exclude]

        ident = nc.alloc_sbuf_tensor("ident", [128, 128], BF16)
        invf = nc.alloc_sbuf_tensor("invf", [128, 1], F32)
        coff = nc.alloc_sbuf_tensor("coff", [128, 1], F32)
        woT_sb = nc.alloc_sbuf_tensor("woT_sb", [128, D], BF16)
        s1_sb = nc.alloc_sbuf_tensor("s1_sb", [128, DH // 128], F32)
        s2_sb = nc.alloc_sbuf_tensor("s2_sb", [128, DH // 128], F32)
        hs1_sb = nc.alloc_sbuf_tensor("hs1_sb", [128, DH // 128], F32)
        hs2_sb = nc.alloc_sbuf_tensor("hs2_sb", [128, DH // 128], F32)
        s3_sb = nc.alloc_sbuf_tensor("s3_sb", [128, D // 128], F32)
        stat = [nc.alloc_sbuf_tensor(f"stat{i}", [128, 12], F32) for i in range(2)]
        mv = [nc.alloc_sbuf_tensor(f"mv{i}", [128, 2], F32) for i in range(2)]
        rstd = [nc.alloc_sbuf_tensor(f"rstd{i}", [128, 1], F32) for i in range(2)]
        nmr = [nc.alloc_sbuf_tensor(f"nmr{i}", [128, 1], F32) for i in range(2)]
        eps_c = nc.alloc_sbuf_tensor("eps_c", [128, 1], F32)
        csh_c = nc.alloc_sbuf_tensor("csh_c", [128, 1], F32)

        g_ms = pl.step("gpsimd", lambda g: g.memset(ident[:], 0.0), inc=sGPC)
        g_ident = pl.step("gpsimd",
                          lambda g: g.affine_select(
                              out=ident[:], in_=ident[:],
                              compare_op=ALU.not_equal, fill=1.0, base=0,
                              pattern=[[-1, 128]], channel_multiplier=1),
                          waits=((sGPC, g_ms),), inc=sGPC)
        pl.step("gpsimd", lambda g: g.memset(eps_c[:], EPS), inc=sGPC)
        g_csts = pl.step("gpsimd", lambda g: g.memset(csh_c[:], -CSHIFT),
                         inc=sGPC)
        g_invf = pl.step("gpsimd",
                         lambda g: g.iota(nmr[0][:], [[0, 1]], base=0,
                                          channel_multiplier=1,
                                          allow_small_or_imprecise_dtypes=True),
                         inc=sGPC)
        a_invf = pl.step("scalar",
                         lambda g: g.activation(invf[0:64, :], nmr[0][0:64, :],
                                                AF.Exp, bias=0.0, scale=-LN1e4_64),
                         waits=((sGPC, g_invf),), inc=sACT)

        for src, dst in ((coff_d, coff), (s1_d, s1_sb), (s2_d, s2_sb),
                         (hs1_d, hs1_sb), (hs2_d, hs2_sb), (s3_d, s3_sb)):
            pl.step("sync",
                    (lambda a, b: lambda g: g.dma_start(b[:], a.ap()))(src, dst),
                    inc=sSET, inc_k=16)
        d_setup = sSET.n

        g_cva, g_ch1, g_ch2 = (16 + 16 + 16) * 16, 64 * 16, 64 * 16

        es_kv = ExitStack()
        q_sb = es_kv.enter_context(nc.sbuf_tensor("q_sb", [128, TLOC], BF16))
        kt_sb = es_kv.enter_context(nc.sbuf_tensor("kt_sb", [128, S], BF16))
        v_sb = es_kv.enter_context(nc.sbuf_tensor("v_sb", [128, NPAN * 130], BF16))
        wq_sb = es_kv.enter_context(nc.sbuf_tensor("wq_sb", [128, 8 * HD], BF16))
        wk_sb = es_kv.enter_context(nc.sbuf_tensor("wk_sb", [128, 8 * HD], BF16))
        wv_sb = es_kv.enter_context(nc.sbuf_tensor("wv_sb", [128, 8 * HD], BF16))
        pl.step("gpsimd", lambda g: g.dma_start(woT_sb[:], woT_d.ap()),
                inc=sWOV, inc_k=16)
        for w_d, w_sb in ((wqT_d, wq_sb), (wkT_d, wk_sb), (wvT_d, wv_sb)):
            pl.step("gpsimd",
                    (lambda a, b: lambda g: g.dma_start(
                        b[:].rearrange("p (i h) -> p i h", h=HD),
                        a.ap().rearrange("(i p) h -> p i h", p=128)))(w_d, w_sb),
                    inc=sWOV, inc_k=16)
        g_wov = sWOV.n

        ev = {}
        es_p1 = ExitStack()
        xc0 = es_p1.enter_context(nc.sbuf_tensor("xc0", [128, 4 * 1024], F32))
        xc1 = es_p1.enter_context(nc.sbuf_tensor("xc1", [128, 4 * 1024], F32))
        lnx0 = es_p1.enter_context(nc.sbuf_tensor("lnx0", [128, 1024], BF16))
        lnx1 = es_p1.enter_context(nc.sbuf_tensor("lnx1", [128, 1024], BF16))
        lxt0 = es_p1.enter_context(nc.sbuf_tensor("lxt0", [128, 8 * P1CH], BF16))
        tabI = es_p1.enter_context(nc.sbuf_tensor("tabI", [128, TLOC], F32))
        tabC = es_p1.enter_context(nc.sbuf_tensor("tabC", [128, TLOC], F32))
        tabS = es_p1.enter_context(nc.sbuf_tensor("tabS", [128, TLOC], F32))
        tsc0 = es_p1.enter_context(nc.sbuf_tensor("tsc0", [128, TLOC], F32))
        tsc1 = es_p1.enter_context(nc.sbuf_tensor("tsc1", [128, TLOC], F32))
        kraw = es_p1.enter_context(nc.sbuf_tensor("kraw", [128, TLOC], F32))
        qraw = es_p1.enter_context(nc.sbuf_tensor("qraw", [128, TLOC], F32))
        ksw = es_p1.enter_context(nc.sbuf_tensor("ksw", [128, TLOC], F32))
        vt_sb = es_p1.enter_context(nc.sbuf_tensor("vt_sb", [128, TLOC], BF16))
        kloc = es_p1.enter_context(nc.sbuf_tensor("kloc", [128, TLOC], BF16))
        vloc = es_p1.enter_context(nc.sbuf_tensor("vloc", [128, 16 * 128], BF16))
        trps0 = es_p1.enter_context(nc.psum_tensor("trps0", [128, 1024], BF16))
        trps1 = es_p1.enter_context(nc.psum_tensor("trps1", [128, 1024], BF16))
        kps0 = es_p1.enter_context(nc.psum_tensor("kps0", [128, P1CH], F32))
        kps1 = es_p1.enter_context(nc.psum_tensor("kps1", [128, P1CH], F32))
        vps = es_p1.enter_context(nc.psum_tensor("vps", [128, P1CH], F32))
        qps = es_p1.enter_context(nc.psum_tensor("qps", [128, P1CH], F32))
        vtps = es_p1.enter_context(nc.psum_tensor("vtps", [128, P1CH], BF16))
        trps = [trps0, trps1]
        kps = [kps0, kps1]
        xc = [xc0, xc1]
        lnx = [lnx0, lnx1]

        tabres = {}

        def table_steps():
            g_iota = pl.step("gpsimd",
                             lambda g: g.iota(tsc0[0:64, :], [[1, TLOC]],
                                              base=0, channel_multiplier=0,
                                              allow_small_or_imprecise_dtypes=True),
                             inc=sGPC)
            yield
            pl.step("vector", lambda g: g.tensor_scalar(
                        tsc1[0:64, :], tsc0[0:64, :], coff[0:64, :], 0.0,
                        ALU.add, ALU.max),
                    waits=((sGPC, g_iota), (sSET, d_setup)), inc=sDVE)
            yield
            pl.step("vector", lambda g: g.tensor_scalar(
                        tsc0[0:64, :], tsc1[0:64, :], invf[0:64, :], None,
                        ALU.mult),
                    waits=((sACT, a_invf),), inc=sDVE)
            yield
            ang_s, ang_c = tsc0, tsc1

            def redux(src, dst_fn):
                pl.step("vector", lambda g: g.tensor_scalar(
                            tabC[0:64, :], src[0:64, :], INV2PI, MAGIC,
                            ALU.mult, ALU.add), inc=sDVE)
                yield
                pl.step("vector", lambda g: g.tensor_single_scalar(
                            tabS[0:64, :], tabC[0:64, :], MAGIC,
                            ALU.subtract), inc=sDVE)
                yield
                pl.step("vector", lambda g: g.scalar_tensor_tensor(
                            dst_fn(), tabS[0:64, :], -C1, src[0:64, :],
                            ALU.mult, ALU.add), inc=sDVE)
                yield
                pl.step("vector", lambda g: g.scalar_tensor_tensor(
                            tabC[0:64, :], tabS[0:64, :], -C2, dst_fn(),
                            ALU.mult, ALU.add), inc=sDVE)
                yield
                pl.step("vector", lambda g: g.scalar_tensor_tensor(
                            dst_fn(), tabS[0:64, :], -C3, tabC[0:64, :],
                            ALU.mult, ALU.add), inc=sDVE)
                yield

            yield from redux(ang_s, lambda: tabI[0:64, :])
            pl.step("vector", lambda g: g.tensor_single_scalar(
                        ang_c[0:64, :], ang_s[0:64, :], math.pi / 2, ALU.add),
                    inc=sDVE)
            yield
            yield from redux(ang_c, lambda: tsc0[0:64, :])
            pl.step("vector", lambda g: g.tensor_scalar(
                        tabS[0:64, :], tabI[0:64, :], PI_LO, -PI_LO,
                        ALU.min, ALU.max), inc=sDVE)
            yield
            v_tabi = pl.step("vector", lambda g: g.tensor_scalar(
                        tabC[0:64, :], tsc0[0:64, :], PI_LO, -PI_LO,
                        ALU.min, ALU.max), inc=sDVE)
            yield
            pl.step("scalar",
                    lambda g: g.activation(tabI[0:64, :], tabS[0:64, :], AF.Sin,
                                           bias=0.0, scale=1.0),
                    waits=((sDVE, v_tabi),), inc=sACT)
            yield
            a_tab = pl.step("scalar",
                            lambda g: g.activation(tsc1[0:64, :], tabC[0:64, :],
                                                   AF.Sin, bias=0.0, scale=1.0),
                            inc=sACT)
            yield
            pl.step("vector", lambda g: g.tensor_copy(tabC[0:64, :], tsc1[0:64, :]),
                    waits=((sACT, a_tab),), inc=sDVE)
            yield
            pl.step("gpsimd", lambda g: g.dma_start(tabC[64:128, :], tsc1[0:64, :]),
                    waits=((sACT, a_tab),), inc=sTAB, inc_k=16)
            yield
            d_ts = pl.step("gpsimd", lambda g: g.dma_start(tabS[64:128, :],
                                                           tabI[0:64, :]),
                           inc=sTAB, inc_k=16)
            yield
            v_tabs = pl.step("vector",
                             lambda g: g.tensor_scalar_mul(
                                 tabS[0:64, :], tabI[0:64, :], -1.0),
                             waits=((sTAB, d_ts),), inc=sDVE)

            tabres['v_tabs'] = v_tabs
            yield

        tgen = table_steps()
        _tdone = object()

        def pump_tables(n):
            for _ in range(n):
                if next(tgen, _tdone) is _tdone:
                    break

        for c in range(NP1):
            slot = c % 2
            w = []
            if c >= 2:
                w.append((sPE, ev[("proj", c - 2)]))
            d_x = pl.step("sync", (lambda cc_, sl: lambda g: g.dma_start(
                        xc[sl][:].rearrange("p (t d) -> p t d", d=D),
                        x_d.ap()[cc_ * P1CH:(cc_ + 1) * P1CH, :]
                        .rearrange("(t p) d -> p t d", p=128)))(c, slot),
                    waits=w, inc=sXC[slot], inc_k=16)
            for tt in range(4):
                par = tt % 2
                xcol = 1024 * tt
                pl.step("vector", (lambda sl, co, pp: lambda g: g.bn_stats(
                            stat[pp][:, 0:6], xc[sl][:, co:co + 512]))(slot, xcol, par),
                        waits=((sXC[slot], d_x),), inc=sDVE)
                pl.step("vector", (lambda sl, co, pp: lambda g: g.bn_stats(
                            stat[pp][:, 6:12], xc[sl][:, co + 512:co + 1024]))(slot, xcol, par),
                        inc=sDVE)
                v_st = pl.step("vector", (lambda pp: lambda g: g.bn_aggr(
                            mv[pp][:], stat[pp][:]))(par), inc=sDVE)
                a_ln = pl.step("scalar", (lambda pp: lambda g: g.activation(
                            rstd[pp][:], mv[pp][:, 1:2], AF.Ln, bias=eps_c[:],
                            scale=1.0))(par),
                        waits=((sDVE, v_st), (sGPC, g_csts)), inc=sACT)
                a_ex = pl.step("scalar", (lambda pp: lambda g: g.activation(
                            rstd[pp][:], rstd[pp][:], AF.Exp, bias=0.0,
                            scale=-0.5))(par),
                        waits=((sACT, a_ln),), inc=sACT)
                v_nm = pl.step("vector", (lambda pp: lambda g: g.scalar_tensor_tensor(
                            nmr[pp][:], mv[pp][:, 0:1], -1.0, rstd[pp][:],
                            ALU.mult, ALU.mult))(par),
                        waits=((sACT, a_ex),), inc=sDVE)
                w = [(sDVE, v_nm)]
                prev_tr = ev.get(("tr1", c, tt - 2)) or ev.get(("tr1", c - 1, tt + 2))
                if prev_tr:
                    w.append((sPE, prev_tr))
                a_nrm = pl.step("scalar", (lambda sl, co, pp: lambda g: g.activation(
                            lnx[pp][:], xc[sl][:, co:co + 1024], AF.Identity,
                            bias=nmr[pp][:], scale=rstd[pp][:]))(slot, xcol, par),
                        waits=w, inc=sACT)
                prev_ev = ev.get(("trv1", c, tt - 2)) or ev.get(("trv1", c - 1, tt + 2))
                for i in range(8):
                    w = [(sACT, a_nrm), (sGPC, g_ident)]
                    if prev_ev:
                        w.append((sDVE, prev_ev))
                    pl.step("tensor", (lambda pp, ii: lambda g: g.transpose(
                                trps[pp][:, 128 * ii:128 * ii + 128],
                                lnx[pp][:, 128 * ii:128 * ii + 128],
                                ident[:]))(par, i),
                            waits=w, inc=(sPE if i == 7 else None))
                ev[("tr1", c, tt)] = sPE.n
                wev = [(sPE, ev[("tr1", c, tt)])]
                if tt == 0 and c >= 1:
                    wev.append((sPE, ev[("proj", c - 1)]))
                ev[("trv1", c, tt)] = pl.step(
                    "vector", (lambda pp, t_: lambda g: g.tensor_copy(
                        lxt0[:].rearrange("p (i t) -> p i t", t=P1CH)[:, :, 128 * t_:128 * t_ + 128],
                        trps[pp][:].rearrange("p (i t) -> p i t", t=128)))(par, tt),
                    waits=wev, inc=sDVE)
                if c >= 1:
                    pump_tables(2)
            v_lxt = ev[("trv1", c, 3)]

            for name, wsb, ps in (("k", wk_sb, kps[c % 2]),
                                  ("v", wv_sb, vps),
                                  ("q", wq_sb, qps)):
                w = [(sDVE, v_lxt), (sWOV, g_wov)]
                if name == "k" and c >= 2:
                    w.append((sACT, ev[("kev", c - 2)]))
                if name in ("v", "q") and c >= 1:
                    w.append((sACT, ev[(name + "ev", c - 1)]))
                for i in range(8):
                    pl.step("tensor", (lambda wsb_, ps_, ii: lambda g: g.matmul(
                                ps_[:], wsb_[:, 128 * ii:128 * ii + 128],
                                lxt0[:, P1CH * ii:P1CH * (ii + 1)],
                                start=(ii == 0), stop=(ii == 7)))(wsb, ps, i),
                            waits=(w if i == 0 else ()),
                            inc=(sPE if i == 7 else None))
                ev[(name + "pe", c)] = sPE.n
            ev[("proj", c)] = sPE.n

            kcol = P1CH * c
            ev[("kev", c)] = pl.step(
                "scalar", (lambda ps_, co: lambda g: g.activation(
                    kraw[:, co:co + P1CH], ps_[:], AF.Copy, bias=0.0,
                    scale=1.0))(kps[c % 2], kcol),
                waits=((sPE, ev[("kpe", c)]),), inc=sACT)
            ev[("vev", c)] = pl.step(
                "scalar", (lambda co: lambda g: g.activation(
                    vt_sb[:, co:co + P1CH], vps[:], AF.Copy, bias=0.0,
                    scale=1.0))(kcol),
                waits=((sPE, ev[("vpe", c)]),), inc=sACT)
            ev[("qev", c)] = pl.step(
                "scalar", (lambda co: lambda g: g.activation(
                    qraw[:, co:co + P1CH], qps[:], AF.Copy, bias=0.0,
                    scale=1.0))(kcol),
                waits=((sPE, ev[("qpe", c)]),), inc=sACT)

            for k in range(4):
                w = [(sACT, ev[("vev", c)])]
                prev = ev.get(("vtv", c, k - 1)) or ev.get(("vtv", c - 1, 3))
                if prev:
                    w.append((sDVE, prev))
                p_vt = pl.step("tensor", (lambda co, kk_: lambda g: g.transpose(
                            vtps[:, 128 * kk_:128 * kk_ + 128],
                            vt_sb[:, co + 128 * kk_:co + 128 * kk_ + 128],
                            ident[:]))(kcol, k),
                        waits=w, inc=sPE)
                ev[("vtv", c, k)] = pl.step(
                    "vector", (lambda cc_, kk_: lambda g: g.tensor_copy(
                        vloc[:, 128 * (4 * cc_ + kk_):128 * (4 * cc_ + kk_) + 128],
                        vtps[:, 128 * kk_:128 * kk_ + 128]))(c, k),
                    waits=((sPE, p_vt),), inc=sDVE)
        v_vloc = ev[("vtv", NP1 - 1, 3)]
        pump_tables(99)

        pl.step("sync", lambda g: g.dma_start(ksw[0:64, :], kraw[64:128, :]),
                waits=((sACT, ev[("kev", NP1 - 1)]),), inc=sKSW, inc_k=16)
        d_ksw = pl.step("sync",
                        lambda g: g.dma_start(ksw[64:128, :], kraw[0:64, :]),
                        inc=sKSW, inc_k=16)
        pl.step("vector", lambda g: g.tensor_mul(tsc0[:], kraw[:], tabC[:]),
                waits=((sKSW, d_ksw), (sDVE, tabres['v_tabs'])), inc=sDVE)
        pl.step("vector", lambda g: g.tensor_mul(tsc1[:], ksw[:], tabS[:]),
                inc=sDVE)
        v_krot = pl.step("vector",
                         lambda g: g.tensor_add(kloc[:], tsc0[:], tsc1[:]),
                         inc=sDVE)
        pl.step("sync", lambda g: g.dma_start(
                    kvs_d.ap()[:, 0:TLOC], kloc[:]),
                waits=((sDVE, v_krot),), inc=sKVST, inc_k=16)
        d_kvst = pl.step("sync", lambda g: g.dma_start(
                    kvs_d.ap()[:, TLOC:2 * TLOC], vloc[:]),
                waits=((sDVE, v_vloc),), inc=sKVST, inc_k=16)
        cc_done = pl.step("gpsimd", lambda g: g.collective_compute(
                    "AllGather", ALU.bypass,
                    replica_groups=[list(range(NCORES))],
                    ins=[kvs_d.ap().opt()], outs=[kvg_d.ap().opt()]),
                waits=((sKVST, d_kvst),), inc=sCC)

        pl.step("sync", lambda g: g.dma_start(ksw[0:64, :], qraw[64:128, :]),
                waits=((sACT, ev[("qev", NP1 - 1)]), (sDVE, v_krot),
                       (sKVST, d_kvst)),
                inc=sKSW, inc_k=16, force_waits=True)
        d_qsw = pl.step("sync",
                        lambda g: g.dma_start(ksw[64:128, :], qraw[0:64, :]),
                        inc=sKSW, inc_k=16)
        pl.step("vector", lambda g: g.tensor_mul(tsc0[:], qraw[:], tabC[:]),
                waits=((sKSW, d_qsw),), inc=sDVE)
        pl.step("vector", lambda g: g.tensor_mul(tsc1[:], ksw[:], tabS[:]),
                inc=sDVE)
        v_qrot = pl.step("vector",
                         lambda g: g.tensor_add(q_sb[:], tsc0[:], tsc1[:]),
                         inc=sDVE)

        if "q" in taps:
            pl.step("sync", lambda g: g.dma_start(
                        tap_d["q"].ap(), q_sb[:].bitcast(U16)),
                    waits=((sDVE, v_qrot),), inc=sTAP, inc_k=16)
        if "kloc" in taps:
            pl.step("sync", lambda g: g.dma_start(
                        tap_d["kloc"].ap(), kloc[:].bitcast(U16)),
                    waits=((sDVE, v_krot),), inc=sTAP, inc_k=16)
        if "vloc" in taps:
            pl.step("sync", lambda g: g.dma_start(
                        tap_d["vloc"].ap(), vloc[:].bitcast(U16)),
                    waits=((sDVE, v_vloc),), inc=sTAP, inc_k=16)
        if "lnx" in taps:
            pl.step("sync", lambda g: g.dma_start(
                        tap_d["lnx"].ap(), lxt0[:].bitcast(U16)),
                    waits=((sDVE, ev[("trv1", 3, 3)]),), inc=sTAP, inc_k=16)

        es_p1.close()
        pl.barrier(sBAR, extra_waits=dma_snapshot())

        d_kt = pl.step("sync", lambda g: g.dma_start(
                    kt_sb[:].rearrange("h (r t) -> h r t", t=TLOC),
                    kvg_d.ap()[:, 0:TLOC].rearrange("(r h) t -> h r t", h=128)),
                waits=((sCC, cc_done),), inc=sKV, inc_k=16)
        for rr in range(NCORES):
            pl.step("scalar", (lambda r_: lambda g: g.dma_start(
                        v_sb[:, 130 * 16 * r_:130 * 16 * (r_ + 1)]
                        .rearrange("p (k c) -> p k c", c=130)[:, :, 0:128],
                        kvg_d.ap()[128 * r_:128 * (r_ + 1), TLOC:2 * TLOC]
                        .rearrange("p (k h) -> p k h", h=128)))(rr),
                    waits=(((sCC, cc_done),) if rr == 0 else ()),
                    inc=sKV2, inc_k=16)
        d_kv = sKV2.n
        g_ones = pl.step("gpsimd", lambda g: g.memset(
                    v_sb[:].rearrange("p (k c) -> p k c", c=130)[:, :, 128:130],
                    1.0),
                waits=((sKV2, d_kv),), inc=sGPC)
        piece = 0

        def conv_gate():
            if piece < 48:
                return (sAPE, piece * 4)
            if piece < 80:
                return (sAPE, 194 + (piece - 48) * 2)
            if piece < 112:
                return (sPACE, (piece - 79))
            return (sPACE, 33 + (piece - 112) * 3 // 8)

        for csrc, cdst, csem, nrow in ((w1T_d, w1b_d, sCVA, 64),
                                       (w2T_d, w2b_d, sCVA, 64),
                                       (w3T_d, w3b_d, sCVA, 256),
                                       (h1T_d, h1b_d, sCVH1, 64),
                                       (h2T_d, h2b_d, sCVH2, 64)):
            nr = csrc.shape[0]
            for r0 in range(0, nr, nrow):
                pl.step("gpsimd",
                        (lambda a, b, r_, n_: lambda g: g.dma_start(
                            b.ap()[r_:r_ + n_, :], a.ap()[r_:r_ + n_, :]))(
                                csrc, cdst, r0, nrow),
                        waits=(conv_gate(),), inc=csem, inc_k=16)
                piece += 1
        assert (sCVA.n, sCVH1.n, sCVH2.n) == (g_cva, g_ch1, g_ch2)

        if "kt" in taps:
            pl.step("sync", lambda g: g.dma_start(
                        tap_d["kt"].ap(), kt_sb[:].bitcast(U16)),
                    waits=((sKV, d_kt),), inc=sTAP, inc_k=16)
        if "v" in taps:
            pl.step("sync", lambda g: g.dma_start(
                        tap_d["v"].ap(), v_sb[:, 0:16 * 130].bitcast(U16)),
                    waits=((sGPC, g_ones),), inc=sTAP, inc_k=16)

        NPR = NPAN // 2
        x1_spill = [0, 0, 0, 0]
        es_at = ExitStack()
        a0 = es_at.enter_context(nc.sbuf_tensor("a0", [128, 1024], BF16))
        a1 = es_at.enter_context(nc.sbuf_tensor("a1", [128, 1024], BF16))
        a2 = es_at.enter_context(nc.sbuf_tensor("a2", [128, 1024], BF16))
        a3 = es_at.enter_context(nc.sbuf_tensor("a3", [128, 1024], BF16))
        o_sb = es_at.enter_context(nc.sbuf_tensor("o_sb", [128, 4 * 128], BF16))
        ot_sb = es_at.enter_context(nc.sbuf_tensor("ot_sb", [128, 512], BF16))
        recl = es_at.enter_context(nc.sbuf_tensor("recl", [128, 4], F32))
        xorb = es_at.enter_context(nc.sbuf_tensor("xorb", [128, 8 * 1024], F32))
        tA0 = es_at.enter_context(nc.sbuf_tensor("tA0", [128, 1024], F32))
        tA1 = es_at.enter_context(nc.sbuf_tensor("tA1", [128, 1024], F32))
        tA2 = es_at.enter_context(nc.sbuf_tensor("tA2", [128, 1024], F32))
        tA3 = es_at.enter_context(nc.sbuf_tensor("tA3", [128, 1024], F32))
        x1h0 = es_at.enter_context(nc.sbuf_tensor("x1h0", [128, 4 * 1024], F32))
        x1h1 = es_at.enter_context(nc.sbuf_tensor("x1h1", [128, 4 * 1024], F32))
        sps0 = es_at.enter_context(nc.psum_tensor("sps0", [128, 1024], F32))
        sps1 = es_at.enter_context(nc.psum_tensor("sps1", [128, 1024], F32))
        ops_t = [es_at.enter_context(nc.psum_tensor(f"ops{i}", [128, 512], F32))
                 for i in range(4)]
        a_sb = [a0, a1, a2, a3]
        sps = [sps0, sps1]
        otps = ops_t[0][:, 0:256].bitcast(BF16)
        tA = [tA0, tA1, tA2, tA3]
        x1h = [x1h0, x1h1]
        aev = {}

        def s_mm(b, i):
            pr = NPR * b + i
            w = [(sKV, d_kt), (sDVE, v_qrot)]
            if pr >= 2:
                w.append((sEXP, pr - 1))
            if i in (0, 1) and b >= 1:
                w.append((sACT, aev[("aff", b - 1, 3 if pr % 2 == 0 else 2)]))
            for half in range(2):
                pl.step("tensor", (lambda ii, hh, par, bb: lambda g: g.matmul(
                            sps[par][:, 512 * hh:512 * hh + 512],
                            kt_sb[:, 128 * (2 * ii + hh):128 * (2 * ii + hh) + 128],
                            q_sb[:, QBLK * bb:QBLK * bb + 512],
                            start=True, stop=True))(i, half, pr % 2, b),
                        waits=(w if half == 0 else ()),
                        inc=(sSPE if half == 1 else None))

        def exp_step(b, i):
            pr = NPR * b + i
            w = [(sSPE, pr + 1), (sGPC, g_csts)]
            if pr >= 4:
                w.append((sAPE, pr - 3))
            pl.step("scalar", (lambda par, spar: lambda g: g.activation(
                        a_sb[par][:], sps[spar][:], AF.Exp,
                        bias=csh_c[:], scale=ISQ))(pr % 4, pr % 2),
                    waits=w, inc=sEXP)

        def av_mm(b, i):
            pr = NPR * b + i
            w = [(sEXP, pr + 1), (sGPC, g_ones)]
            if i == 0 and b >= 1:
                w.append((sDVE, aev[("otv_last", b - 1)]))
                w.append((sACT, aev[("oev_last", b - 1)]))
            for s in range(4):
                for j in range(2):
                    p = 2 * i + j
                    pl.step("tensor", (lambda par, pp, jj, s_: lambda g: g.matmul(
                                ops_t[s_][:, 0:129],
                                a_sb[par][:, 512 * jj + 128 * s_:512 * jj + 128 * s_ + 128],
                                v_sb[:, 130 * pp:130 * pp + 129],
                                start=(pp == 0), stop=(pp == NPAN - 1)))(pr % 4, p, j, s),
                            waits=(w if (s == 0 and j == 0) else ()),
                            inc=(sAPE if (s == 3 and j == 1) else None))

        def xor_slot(b, s):
            return xorb[:, 1024 * ((b % 2) * 4 + s):1024 * ((b % 2) * 4 + s) + 1024]

        def emit_xor_loads(b):
            for s in range(4):
                w = []
                if b >= 2:
                    w.append((sDVE, aev[("x1", b - 2, s)]))
                aev[("xord", b, s)] = pl.step(
                    "sync", (lambda bb, s_: lambda g: g.dma_start(
                        xor_slot(bb, s_),
                        x_d.ap()[QBLK * bb + 128 * s_:QBLK * bb + 128 * s_ + 128, :]))(b, s),
                    waits=w, inc=sXOR[(b % 2) * 4 + s], inc_k=16)

        for b in range(NQB):
            emit_xor_loads(b)
            s_mm(b, 0)
            s_mm(b, 1)
            for i in range(NPR):
                exp_step(b, i)
                av_mm(b, i)
                if i + 2 < NPR:
                    s_mm(b, i + 2)
            ape_blk = sAPE.n

            for s in range(4):
                w = [(sAPE, ape_blk)]
                if b >= 1:
                    w.append((sACT, aev[("oev_last", b - 1)]))
                aev[("rl", b, s)] = pl.step(
                    "vector", (lambda s_: lambda g: g.reciprocal(
                        recl[:, s_:s_ + 1], ops_t[s_][:, 128:129]))(s),
                    waits=w, inc=sDVE)
                w = [(sDVE, aev[("rl", b, s)])]
                if b >= 1:
                    w.append((sPE, aev[("otr_last", b - 1)]))
                aev[("oev", b, s)] = pl.step(
                    "scalar", (lambda s_: lambda g: g.activation(
                        o_sb[:, 128 * s_:128 * s_ + 128],
                        ops_t[s_][:, 0:128], AF.Copy,
                        bias=0.0, scale=recl[:, s_:s_ + 1]))(s),
                    waits=w, inc=sACT)
                w = [(sACT, aev[("oev", b, s)])]
                if b >= 1:
                    w.append((sDVE, aev[("otv_last", b - 1)]))
                p_ot = pl.step("tensor", (lambda s_: lambda g: g.transpose(
                            otps[:, 128 * s_:128 * s_ + 128],
                            o_sb[:, 128 * s_:128 * s_ + 128], ident[:]))(s),
                        waits=w, inc=sPE)
                w = [(sPE, p_ot)]
                if b >= 1:
                    w.append((sPE, aev[("ype", b - 1, 3)]))
                aev[("otv", b, s)] = pl.step(
                    "vector", (lambda s_: lambda g: g.tensor_copy(
                        ot_sb[:, 128 * s_:128 * s_ + 128],
                        otps[:, 128 * s_:128 * s_ + 128]))(s),
                    waits=w, inc=sDVE)
            aev[("oev_last", b)] = aev[("oev", b, 3)]
            aev[("otv_last", b)] = aev[("otv", b, 3)]
            aev[("otr_last", b)] = sPE.n

            for s in range(4):
                par = s % 2
                w = [(sDVE, aev[("otv", b, s)]),
                     (sEXP, (b + 1) * NPR), (sWOV, g_wov)]
                if s >= 2:
                    w.append((sACT, aev[("aff", b, s - 2)]))
                for half in range(2):
                    pl.step("tensor", (lambda s_, pp, hh_: lambda g: g.matmul(
                                sps[pp][:, 512 * hh_:512 * hh_ + 512],
                                ot_sb[:, 128 * s_:128 * s_ + 128],
                                woT_sb[:, 512 * hh_:512 * hh_ + 512],
                                start=True, stop=True))(s, par, half),
                            waits=(w if half == 0 else ()),
                            inc=(sPE if half == 1 else None))
                aev[("ype", b, s)] = sPE.n
                pl.step("vector", (lambda pp: lambda g: g.bn_stats(
                            stat[pp][:, 0:6], sps[pp][:, 0:512]))(par),
                        waits=((sPE, aev[("ype", b, s)]),), inc=sDVE)
                pl.step("vector", (lambda pp: lambda g: g.bn_stats(
                            stat[pp][:, 6:12], sps[pp][:, 512:1024]))(par),
                        inc=sDVE)
                v_st = pl.step("vector", (lambda pp: lambda g: g.bn_aggr(
                            mv[pp][:], stat[pp][:]))(par), inc=sDVE)
                a_ln = pl.step("scalar", (lambda pp: lambda g: g.activation(
                            rstd[pp][:], mv[pp][:, 1:2], AF.Ln, bias=eps_c[:],
                            scale=1.0))(par),
                        waits=((sDVE, v_st), (sGPC, g_csts)), inc=sACT)
                a_ex = pl.step("scalar", (lambda pp: lambda g: g.activation(
                            rstd[pp][:], rstd[pp][:], AF.Exp, bias=0.0,
                            scale=-0.5))(par),
                        waits=((sACT, a_ln),), inc=sACT)
                v_nm = pl.step("vector", (lambda pp: lambda g: g.scalar_tensor_tensor(
                            nmr[pp][:], mv[pp][:, 0:1], -1.0, rstd[pp][:],
                            ALU.mult, ALU.mult))(par),
                        waits=((sACT, a_ex),), inc=sDVE)
                w = [(sDVE, v_nm)]
                prev_x1 = aev.get(("x1", b - 1, s))
                if prev_x1:
                    w.append((sDVE, prev_x1))
                aev[("aff", b, s)] = pl.step(
                    "scalar", (lambda s_: lambda g: g.activation(
                        tA[s_][:], sps[s_ % 2][:], AF.Identity,
                        bias=nmr[s_ % 2][:], scale=rstd[s_ % 2][:]))(s),
                    waits=w, inc=sACT)

            for s in range(4):
                w = [(sACT, aev[("aff", b, s)]),
                     (sXOR[(b % 2) * 4 + s], aev[("xord", b, s)])]
                if b >= 2 and s < 2:
                    w.append((sSP[b - 2], x1_spill[b - 2]))
                aev[("x1", b, s)] = pl.step(
                    "vector", (lambda s_, bb: lambda g: g.tensor_add(
                        x1h[bb % 2][:, 1024 * s_:1024 * (s_ + 1)], tA[s_][:],
                        xor_slot(bb, s_)))(s, b),
                    waits=w, inc=sDVE)
            x1_spill[b] = pl.step(
                "sync", (lambda bb: lambda g: g.dma_start(
                    x1_d.ap()[QBLK * bb:QBLK * (bb + 1), :]
                    .rearrange("(t p) d -> p t d", p=128),
                    x1h[bb % 2][:].rearrange("p (t d) -> p t d", d=D)))(b),
                waits=((sDVE, aev[("x1", b, 3)]),), inc=sSP[b], inc_k=16)

        if "x1" in taps:
            pl.step("sync", lambda g: g.dma_start(
                        tap_d["x1"].ap(), x1_d.ap()),
                    waits=tuple((sSP[b], 16) for b in range(4)),
                    inc=sTAP, inc_k=16)

        es_at.close()
        es_kv.close()
        pl.barrier(sBAR, extra_waits=dma_snapshot(
            exclude=("s_cva", "s_cvh1", "s_cvh2")) + [(sGPC, sGPC.n)],
            engines=("tensor", "vector", "scalar", "sync"))

        es_f = ExitStack()
        x1a0 = es_f.enter_context(nc.sbuf_tensor("x1a0", [128, 1024], F32))
        x1a1 = es_f.enter_context(nc.sbuf_tensor("x1a1", [128, 1024], F32))
        shA = es_f.enter_context(nc.sbuf_tensor("shA", [128, 1024], F32))
        lxt2 = es_f.enter_context(nc.sbuf_tensor("lxt2", [128, 8 * FCH], BF16))
        Hbuf = es_f.enter_context(nc.sbuf_tensor("Hbuf", [128, 32 * FCH], BF16))
        G1buf = es_f.enter_context(nc.sbuf_tensor("G1buf", [128, 32 * FCH], BF16))
        t1sil0 = es_f.enter_context(nc.sbuf_tensor("t1sil0", [128, FCH], BF16))
        t1sil1 = es_f.enter_context(nc.sbuf_tensor("t1sil1", [128, FCH], BF16))
        mida = es_f.enter_context(nc.sbuf_tensor("mida", [128, 2 * 32 * 256], BF16))
        x2b0 = es_f.enter_context(nc.sbuf_tensor("x2b0", [128, 1024], F32))
        x2b1 = es_f.enter_context(nc.sbuf_tensor("x2b1", [128, 1024], F32))
        pgA = es_f.enter_context(nc.psum_tensor("pgA", [128, 1024], F32))
        pgB = es_f.enter_context(nc.psum_tensor("pgB", [128, 1024], F32))
        pgC = es_f.enter_context(nc.psum_tensor("pgC", [128, 1024], F32))
        pgD = es_f.enter_context(nc.psum_tensor("pgD", [128, 1024], F32))
        x1a = [x1a0, x1a1]
        x1b = x1a
        lnx2 = [shA[:, 0:512].bitcast(BF16), shA[:, 512:1024].bitcast(BF16)]
        ft3 = shA[:]
        YT = G1buf[:, 0:8 * FCH]
        t1sil = [t1sil0, t1sil1]
        x2b = [x2b0, x2b1]
        pg = [pgA, pgB, pgC, pgD]
        trf = [pgA[:, 0:512].bitcast(BF16), pgB[:, 0:512].bitcast(BF16)]
        ytk = [pgC[:, 0:512].bitcast(BF16), pgD[:, 0:512].bitcast(BF16)]


        def w12_slab(mat, grp):
            base = ((grp % 2) * 2 + mat) * 8 * 256
            return mida[:, base:base + 8 * 256]

        def mid_slot(pair):
            base = (pair % 2) * 32 * 256
            return mida[:, base:base + 32 * 256]

        for c in range(NFCH):
            tc0 = FCH * c

            for s in range(8):
                par = s % 2
                w = [(sSP[(tc0 + 128 * s) // QBLK], 16)]
                prev_n = ev.get(("nrm2", c, s - 2)) or ev.get(("nrm2", c - 1, s + 6))
                if prev_n:
                    w.append((sACT, prev_n))
                if c >= 1:
                    w.append((sDVE, ev[("x2", c - 1, 6 + par)]))
                d_x1 = pl.step("sync", (lambda r0, pp: lambda g: g.dma_start(
                            x1a[pp][:], x1_d.ap()[r0:r0 + 128, :]))(tc0 + 128 * s, par),
                        waits=w, inc=sX1[par], inc_k=16)
                pl.step("vector", (lambda pp: lambda g: g.bn_stats(
                            stat[pp][:, 0:6], x1a[pp][:, 0:512]))(par),
                        waits=((sX1[par], d_x1),), inc=sDVE)
                pl.step("vector", (lambda pp: lambda g: g.bn_stats(
                            stat[pp][:, 6:12], x1a[pp][:, 512:1024]))(par),
                        inc=sDVE)
                v_st = pl.step("vector", (lambda pp: lambda g: g.bn_aggr(
                            mv[pp][:], stat[pp][:]))(par), inc=sDVE)
                a_ln = pl.step("scalar", (lambda pp: lambda g: g.activation(
                            rstd[pp][:], mv[pp][:, 1:2], AF.Ln, bias=eps_c[:],
                            scale=1.0))(par),
                        waits=((sDVE, v_st), (sGPC, g_csts)), inc=sACT)
                a_ex = pl.step("scalar", (lambda pp: lambda g: g.activation(
                            rstd[pp][:], rstd[pp][:], AF.Exp, bias=0.0,
                            scale=-0.5))(par),
                        waits=((sACT, a_ln),), inc=sACT)
                v_nm = pl.step("vector", (lambda pp: lambda g: g.scalar_tensor_tensor(
                            nmr[pp][:], mv[pp][:, 0:1], -1.0, rstd[pp][:],
                            ALU.mult, ALU.mult))(par),
                        waits=((sACT, a_ex),), inc=sDVE)
                w = [(sDVE, v_nm)]
                prev_t = ev.get(("tr2", c, s - 2)) or ev.get(("tr2", c - 1, s + 6))
                if prev_t:
                    w.append((sPE, prev_t))
                ev[("nrm2", c, s)] = pl.step(
                    "scalar", (lambda pp: lambda g: g.activation(
                        lnx2[pp], x1a[pp][:], AF.Identity,
                        bias=nmr[pp][:], scale=rstd[pp][:]))(par),
                    waits=w, inc=sACT)
                prev_tv = ev.get(("trv2", c, s - 2)) or ev.get(("trv2", c - 1, s + 6))
                for i in range(8):
                    w = [(sACT, ev[("nrm2", c, s)])]
                    if prev_tv:
                        w.append((sDVE, prev_tv))
                    if s < 2 and c >= 1:
                        w.append((sACT, ev[("aff3", c - 1, 7)]))
                        w.append((sPE, ev[("w12pe", c - 1)]))
                    pl.step("tensor", (lambda pp, ii: lambda g: g.transpose(
                                trf[pp][:, 128 * ii:128 * ii + 128],
                                lnx2[pp][:, 128 * ii:128 * ii + 128],
                                ident[:]))(par, i),
                            waits=w, inc=(sPE if i == 7 else None))
                ev[("tr2", c, s)] = sPE.n
                ev[("trv2", c, s)] = pl.step(
                    "vector", (lambda pp, s_: lambda g: g.tensor_copy(
                        lxt2[:].rearrange("p (i t) -> p i t", t=FCH)[:, :, 128 * s_:128 * s_ + 128],
                        trf[pp].rearrange("p (i t) -> p i t", t=128)))(par, s),
                    waits=((sPE, ev[("tr2", c, s)]),), inc=sDVE)
            v_lxt2 = ev[("trv2", c, 7)]

            def emit_w12_load(grp):
                w = []
                if grp >= 2:
                    w.append((sPE, ev[("bpe", c, 2 * (grp - 2) + 1)]))
                elif c >= 1:
                    w.append((sPE, ev[("epe", c - 1, 7)]))
                if grp == 0 and c == 0:
                    w.append((sCVA, g_cva))
                for mat, wsrc in ((0, w1b_d), (1, w2b_d)):
                    for i in range(8):
                        pl.step("sync", (lambda ws, gg, mm, ii: lambda g: g.dma_start(
                                    w12_slab(mm, gg)[:, 256 * ii:256 * (ii + 1)],
                                    ws.ap()[128 * ii:128 * (ii + 1),
                                            256 * gg:256 * (gg + 1)]))(wsrc, grp, mat, i),
                                waits=(w if (mat == 0 and i == 0) else ()),
                                inc=sW12[grp % 2], inc_k=16)
                ev[("w12d", c, grp)] = sW12[grp % 2].n

            emit_w12_load(0)
            emit_w12_load(1)
            for j in range(32):
                grp = j // 2
                if j % 2 == 0 and grp >= 1 and grp + 1 < 16:
                    emit_w12_load(grp + 1)
                ps1, ps2 = pg[2 * (j % 2)], pg[2 * (j % 2) + 1]
                w = [(sDVE, v_lxt2), (sW12[grp % 2], ev[("w12d", c, grp)])]
                if j >= 2:
                    w.append((sACT, ev[("bsil", c, j - 2)]))
                    w.append((sDVE, ev[("bh", c, j - 2)]))
                else:
                    w.append((sDVE, ev[("trv2", c, 6 + j)]))
                    if c >= 1:
                        w.append((sACT, ev[("aff3", c - 1, 7)]))
                    if j == 0 and c >= 1:
                        w.append((sPE, ev[("epe", c - 1, 7)]))
                for mat, ps in ((0, ps1), (1, ps2)):
                    for i in range(8):
                        for half in range(2):
                            pl.step("tensor",
                                    (lambda mm, jj, ii, hf, ps_: lambda g: g.matmul(
                                        ps_[:, 512 * hf:512 * hf + 512],
                                        w12_slab(mm, jj // 2)[:, 256 * ii + 128 * (jj % 2):256 * ii + 128 * (jj % 2) + 128],
                                        lxt2[:, FCH * ii + 512 * hf:FCH * ii + 512 * hf + 512],
                                        start=(ii == 0), stop=(ii == 7)))(mat, j, i, half, ps),
                                    waits=(w if (mat == 0 and i == 0 and half == 0) else ()),
                                    inc=(sPE if (mat == 1 and i == 7 and half == 1) else None))
                ev[("bpe", c, j)] = sPE.n
                if c == 0:
                    pl.step("tensor", lambda g: g.nop(), inc=sPACE)
                ev[("bsil", c, j)] = pl.step(
                    "scalar", (lambda jj, ps_: lambda g: g.activation(
                        t1sil[jj % 2][:], ps_[:], AF.Silu, bias=0.0,
                        scale=s1_sb[:, jj:jj + 1]))(j, ps1),
                    waits=((sPE, ev[("bpe", c, j)]),), inc=sACT)
                ev[("bh", c, j)] = pl.step(
                    "vector", (lambda jj, ps_: lambda g: g.scalar_tensor_tensor(
                        Hbuf[:, FCH * jj:FCH * (jj + 1)], ps_[:],
                        s2_sb[:, jj:jj + 1], t1sil[jj % 2][:],
                        ALU.mult, ALU.mult))(j, ps2),
                    waits=((sACT, ev[("bsil", c, j)]),), inc=sDVE)
            ev[("w12pe", c)] = ev[("bpe", c, 31)]
            v_H = ev[("bh", c, 31)]

            def middle(layer, wsrc, src_buf, dst_buf, scale_sb, conv_sem, conv_n):
                def emit_mid_load(pair):
                    w = [(conv_sem, conv_n)]
                    if pair >= 2:
                        w.append((sPE, ev[("mpe", layer, c, 2 * (pair - 2) + 1)]))
                    elif layer == 1:
                        w.append((sPE, ev[("mpe", 0, c, 2 * (pair + 14) + 1)]))
                    elif pair == 0:
                        w.append((sPE, ev[("bpe", c, 31)]))
                    elif c >= 1:
                        w.append((sPE, ev[("epe", c - 1, 7)]))
                    for hh in range(32):
                        pl.step("sync", (lambda pp, h_: lambda g: g.dma_start(
                                    mid_slot(pp)[:, 256 * h_:256 * (h_ + 1)],
                                    wsrc.ap()[128 * h_:128 * (h_ + 1),
                                              256 * pp:256 * (pp + 1)]))(pair, hh),
                                waits=(w if hh == 0 else ()),
                                inc=sMID[pair % 2], inc_k=16)
                    ev[("midd", layer, c, pair)] = sMID[pair % 2].n

                emit_mid_load(0)
                emit_mid_load(1)
                for o in range(32):
                    pair = o // 2
                    if o % 2 == 0 and pair >= 1 and pair + 1 < 16:
                        emit_mid_load(pair + 1)
                    ps = pg[o % 4]
                    w = [(sMID[pair % 2], ev[("midd", layer, c, pair)])]
                    if layer == 0:
                        w.append((sDVE, v_H))
                    else:
                        w.append((sACT, ev[("gel", 0, c, 31)]))
                    if o >= 4:
                        w.append((sACT, ev[("gel", layer, c, o - 4)]))
                    elif layer == 0:
                        w.append((sACT, ev[("bsil", c, 30 + (o % 2))]))
                        w.append((sDVE, ev[("bh", c, 30 + (o % 2))]))
                    else:
                        w.append((sACT, ev[("gel", 0, c, 28 + o)]))
                    for hh in range(32):
                        for half in range(2):
                            pl.step("tensor",
                                    (lambda oo, h_, hf, ps_, pp: lambda g: g.matmul(
                                        ps_[:, 512 * hf:512 * hf + 512],
                                        mid_slot(pp)[:, 256 * h_ + 128 * (oo % 2):256 * h_ + 128 * (oo % 2) + 128],
                                        src_buf[:, FCH * h_ + 512 * hf:FCH * h_ + 512 * hf + 512],
                                        start=(h_ == 0), stop=(h_ == 31)))(o, hh, half, ps, pair),
                                    waits=(w if (hh == 0 and half == 0) else ()),
                                    inc=(sPE if (hh == 31 and half == 1) else None))
                    ev[("mpe", layer, c, o)] = sPE.n
                    if c == 0 and layer == 0:
                        pl.step("tensor", lambda g: g.nop(), inc=sPACE)
                    wge = [(sPE, ev[("mpe", layer, c, o)])]
                    if layer == 0 and o < 8 and c >= 1:
                        wge.append((sPE, ev[("ftr", c - 1, 7)]))
                    ev[("gel", layer, c, o)] = pl.step(
                        "scalar", (lambda oo, ps_: lambda g: g.activation(
                            dst_buf[:, FCH * oo:FCH * (oo + 1)], ps_[:],
                            AF.Gelu, bias=0.0,
                            scale=scale_sb[:, oo:oo + 1]))(o, ps),
                        waits=wge, inc=sACT)

            middle(0, h1b_d, Hbuf, G1buf, hs1_sb, sCVH1, g_ch1)
            middle(1, h2b_d, G1buf, Hbuf, hs2_sb, sCVH2, g_ch2)

            def emit_w3_load(pair):
                w = [(sCVA, g_cva)]
                if pair >= 2:
                    w.append((sPE, ev[("epe", c, 2 * (pair - 2) + 1)]))
                else:
                    w.append((sPE, ev[("mpe", 1, c, 2 * (pair + 14) + 1)]))
                for hh in range(32):
                    pl.step("sync", (lambda pp, h_: lambda g: g.dma_start(
                                mid_slot(pp)[:, 256 * h_:256 * (h_ + 1)],
                                w3b_d.ap()[128 * h_:128 * (h_ + 1),
                                           256 * pp:256 * (pp + 1)]))(pair, hh),
                            waits=(w if hh == 0 else ()),
                            inc=sMID[pair % 2], inc_k=16)
                ev[("w3d", c, pair)] = sMID[pair % 2].n

            emit_w3_load(0)
            emit_w3_load(1)
            for dd in range(8):
                pair = dd // 2
                if dd % 2 == 0 and pair >= 1 and pair + 1 < 4:
                    emit_w3_load(pair + 1)
                ps = pg[dd % 4]
                w = [(sMID[pair % 2], ev[("w3d", c, pair)]),
                     (sACT, ev[("gel", 1, c, 31)])]
                if dd >= 4:
                    w.append((sACT, ev[("eev", c, dd - 4)]))
                else:
                    w.append((sACT, ev[("gel", 1, c, 28 + dd)]))
                for hh in range(32):
                    for half in range(2):
                        pl.step("tensor",
                                (lambda d_, h_, hf, ps_, pp: lambda g: g.matmul(
                                    ps_[:, 512 * hf:512 * hf + 512],
                                    mid_slot(pp)[:, 256 * h_ + 128 * (d_ % 2):256 * h_ + 128 * (d_ % 2) + 128],
                                    Hbuf[:, FCH * h_ + 512 * hf:FCH * h_ + 512 * hf + 512],
                                    start=(h_ == 0), stop=(h_ == 31)))(dd, hh, half, ps, pair),
                                waits=(w if (hh == 0 and half == 0) else ()),
                                inc=(sPE if (hh == 31 and half == 1) else None))
                ev[("epe", c, dd)] = sPE.n
                w = [(sPE, ev[("epe", c, dd)])]
                if c >= 1 and dd == 0:
                    w.append((sPE, ev[("ftr", c - 1, 7)]))
                ev[("eev", c, dd)] = pl.step(
                    "scalar", (lambda d_, ps_: lambda g: g.activation(
                        YT[:, FCH * d_:FCH * (d_ + 1)], ps_[:], AF.Copy,
                        bias=0.0, scale=s3_sb[:, d_:d_ + 1]))(dd, ps),
                    waits=w, inc=sACT)

            for s in range(8):
                par = s % 2
                w = [(sACT, ev[("nrm2", c, 7)])]
                prev_x2 = ev.get(("x2", c, s - 2)) or ev.get(("x2", c - 1, s + 6))
                if prev_x2:
                    w.append((sDVE, prev_x2))
                d_x1b = pl.step("sync", (lambda r0, pp: lambda g: g.dma_start(
                            x1b[pp][:], x1_d.ap()[r0:r0 + 128, :]))(tc0 + 128 * s, par),
                        waits=w, inc=sX1[par], inc_k=16)
                prev_aff3 = ev.get(("aff3", c, s - 2))
                for i in range(8):
                    w = [(sACT, ev[("eev", c, 7)])]
                    if prev_aff3:
                        w.append((sACT, prev_aff3))
                    pl.step("tensor", (lambda pp, ii, s_: lambda g: g.transpose(
                                ytk[pp][:, 128 * ii:128 * ii + 128],
                                YT[:, FCH * ii + 128 * s_:FCH * ii + 128 * s_ + 128],
                                ident[:]))(par, i, s),
                            waits=w, inc=(sPE if i == 7 else None))
                ev[("ftr", c, s)] = sPE.n
                pl.step("vector", (lambda pp: lambda g: g.bn_stats(
                            stat[pp][:, 0:6], ytk[pp][:, 0:512]))(par),
                        waits=((sPE, ev[("ftr", c, s)]),), inc=sDVE)
                pl.step("vector", (lambda pp: lambda g: g.bn_stats(
                            stat[pp][:, 6:12], ytk[pp][:, 512:1024]))(par),
                        inc=sDVE)
                v_st3 = pl.step("vector", (lambda pp: lambda g: g.bn_aggr(
                            mv[pp][:], stat[pp][:]))(par), inc=sDVE)
                a_l3 = pl.step("scalar", (lambda pp: lambda g: g.activation(
                            rstd[pp][:], mv[pp][:, 1:2], AF.Ln, bias=eps_c[:],
                            scale=1.0))(par),
                        waits=((sDVE, v_st3), (sGPC, g_csts)), inc=sACT)
                a_e3 = pl.step("scalar", (lambda pp: lambda g: g.activation(
                            rstd[pp][:], rstd[pp][:], AF.Exp, bias=0.0,
                            scale=-0.5))(par),
                        waits=((sACT, a_l3),), inc=sACT)
                v_n3 = pl.step("vector", (lambda pp: lambda g: g.scalar_tensor_tensor(
                            nmr[pp][:], mv[pp][:, 0:1], -1.0, rstd[pp][:],
                            ALU.mult, ALU.mult))(par),
                        waits=((sACT, a_e3),), inc=sDVE)
                w = [(sDVE, v_n3)]
                prev = ev.get(("x2", c, s - 1)) or ev.get(("x2", c - 1, 7))
                if prev:
                    w.append((sDVE, prev))
                ev[("aff3", c, s)] = pl.step(
                    "scalar", (lambda pp: lambda g: g.activation(
                        ft3, ytk[pp][:], AF.Identity,
                        bias=nmr[pp][:], scale=rstd[pp][:]))(par),
                    waits=w, inc=sACT)
                w = [(sACT, ev[("aff3", c, s)]), (sX1[par], d_x1b)]
                if sOUT[par].n > 0:
                    w.append((sOUT[par], sOUT[par].n))
                ev[("x2", c, s)] = pl.step(
                    "vector", (lambda pp: lambda g: g.tensor_add(
                        x2b[pp][:], ft3, x1b[pp][:]))(par),
                    waits=w, inc=sDVE)
                pl.step("sync", (lambda r0, pp: lambda g: g.dma_start(
                            out_d.ap()[r0:r0 + 128, :], x2b[pp][:]))(tc0 + 128 * s, par),
                        waits=((sDVE, ev[("x2", c, s)]),), inc=sOUT[par], inc_k=16)

        es_f.close()
        final_waits = [(sOUT[0], sOUT[0].n), (sOUT[1], sOUT[1].n)]
        if taps:
            final_waits.append((sTAP, sTAP.n))
        pl.step("sync", None, waits=tuple(final_waits))

        pl.emit(block)
    es_sem.close()
    return nc


_BUILT = {}


def _get_program(taps=()):
    key = tuple(sorted(taps))
    if key not in _BUILT:
        _BUILT[key] = build_program(taps=key)
    return _BUILT[key]


def _prep_inputs(kw):
    x = np.asarray(kw["L0_attn_x"]).reshape(S, D)
    wq = np.asarray(kw["L0_attn_wq"])
    wk = np.asarray(kw["L0_attn_wk"])
    wv = np.asarray(kw["L0_attn_wv"])
    wo = np.asarray(kw["L0_attn_wo"])
    perm = np.concatenate([np.arange(0, HD, 2), np.arange(1, HD, 2)])
    shared = dict(
        wqT=np.ascontiguousarray(wq[perm].T),
        wkT=np.ascontiguousarray(wk[perm].T),
        wvT=np.ascontiguousarray(wv.T),
        woT=np.ascontiguousarray(wo.T),
        w1T=np.ascontiguousarray(np.asarray(kw["L0_ffn_w1_w"]).T),
        w2T=np.ascontiguousarray(np.asarray(kw["L0_ffn_w2_w"]).T),
        h1T=np.ascontiguousarray(np.asarray(kw["L0_ffn_h1_w"]).T),
        h2T=np.ascontiguousarray(np.asarray(kw["L0_ffn_h2_w"]).T),
        w3T=np.ascontiguousarray(np.asarray(kw["L0_ffn_w3_w"]).T),
        s1c=np.ascontiguousarray(np.asarray(kw["L0_ffn_w1_s"]).reshape(DH // 128, 128).T),
        s2c=np.ascontiguousarray(np.asarray(kw["L0_ffn_w2_s"]).reshape(DH // 128, 128).T),
        hs1c=np.ascontiguousarray(np.asarray(kw["L0_ffn_h1_s"]).reshape(DH // 128, 128).T),
        hs2c=np.ascontiguousarray(np.asarray(kw["L0_ffn_h2_s"]).reshape(DH // 128, 128).T),
        s3c=np.ascontiguousarray(np.asarray(kw["L0_ffn_w3_s"]).reshape(D // 128, 128).T),
    )
    in_maps = []
    for c in range(NCORES):
        m = dict(shared)
        m["x"] = np.ascontiguousarray(x[c * TLOC:(c + 1) * TLOC])
        m["coffm1"] = np.full((128, 1), float(c * TLOC - 1), dtype=np.float32)
        in_maps.append(m)
    return in_maps


def kernel(**kw):
    nc = _get_program(())
    in_maps = _prep_inputs(kw)
    res = run_bass_kernel_spmd(nc, in_maps, core_ids=list(range(NCORES)))
    out = np.concatenate([res.results[c]["out"] for c in range(NCORES)], axis=0)
    return out.reshape(1, S, D)


# version 29
# speedup vs baseline: 1.0761x; 1.0761x over previous
import math
from contextlib import ExitStack
import numpy as np
import concourse.bass as bass
import concourse.mybir as mybir
from concourse.bass_utils import run_bass_kernel_spmd

F32 = mybir.dt.float32
BF16 = mybir.dt.bfloat16
U16 = mybir.dt.uint16
AF = mybir.ActivationFunctionType
ALU = mybir.AluOpType

NCORES = 8
S = 16384
D = 1024
HD = 128
DH = 4096
TLOC = S // NCORES
P1CH = 512
NP1 = TLOC // P1CH
QBLK = 512
NQB = TLOC // QBLK
FCH = 1024
NFCH = TLOC // FCH
NPAN = S // 128
EPS = 1e-5
ISQ = 1.0 / math.sqrt(HD)
CSHIFT = 18.0
MAGIC = float(np.float32(1.5 * 2 ** 23))
INV2PI = float(np.float32(1.0 / (2 * math.pi)))
_C1 = np.float32(6.28125)
_C2 = np.float32(2 * math.pi - float(_C1))
_C3 = np.float32(2 * math.pi - float(_C1) - float(_C2))
C1, C2, C3 = float(_C1), float(_C2), float(_C3)
LN1e4_64 = math.log(10000.0) / 64.0
PI_LO = float(np.nextafter(np.float32(np.pi), np.float32(0.0)))

ENGINES = ("tensor", "vector", "scalar", "gpsimd", "sync")


class Sem:
    def __init__(self, handle, name):
        self.h = handle
        self.name = name
        self.n = 0

    def bump(self, k=1):
        self.n += k
        return self.n


class Plan:
    def __init__(self):
        self.steps = {e: [] for e in ENGINES}
        self.hwm = {e: {} for e in ENGINES}
        self.self_sems = {}

    def step(self, eng, fn, waits=(), inc=None, inc_k=1, force_waits=False):
        ss = self.self_sems.get(eng)
        if ss is not None and ss.n > 0:
            waits = tuple(waits) + ((ss, ss.n),)
        w = []
        for sem, val in waits:
            if sem is None or val is None or val <= 0:
                continue
            if not force_waits and self.hwm[eng].get(sem.name, 0) >= val:
                continue
            self.hwm[eng][sem.name] = max(self.hwm[eng].get(sem.name, 0), val)
            w.append((sem, val))
        self.steps[eng].append((w, fn, inc, inc_k))
        if inc is not None:
            return inc.bump(inc_k)
        return None

    def barrier(self, sbar, extra_waits=(), engines=ENGINES):
        base = sbar.n
        for e in engines:
            self.step(e, lambda g: g.nop(), inc=sbar)
        for e in engines:
            self.step(e, None,
                      waits=tuple(extra_waits) + ((sbar, base + len(engines)),))

    def emit(self, block):
        def mk(eng_name):
            steps = self.steps[eng_name]

            def body(g):
                for w, fn, inc, inc_k in steps:
                    for sem, val in w:
                        g.wait_ge(sem.h, val)
                    if fn is None:
                        continue
                    ins = fn(g)
                    if inc is not None:
                        assert ins is not None, f"inc step returned None ({eng_name})"
                        ins.then_inc(inc.h, inc_k)
            return body
        block.tensor(mk("tensor"))
        block.vector(mk("vector"))
        block.scalar(mk("scalar"))
        block.gpsimd(mk("gpsimd"))
        block.sync(mk("sync"))


def build_program(taps=()):
    nc = bass.Bass("TRN2", target_bir_lowering=False)

    x_d = nc.dram_tensor("x", [TLOC, D], F32, kind="ExternalInput")
    wqT_d = nc.dram_tensor("wqT", [D, HD], F32, kind="ExternalInput")
    wkT_d = nc.dram_tensor("wkT", [D, HD], F32, kind="ExternalInput")
    wvT_d = nc.dram_tensor("wvT", [D, HD], F32, kind="ExternalInput")
    woT_d = nc.dram_tensor("woT", [HD, D], F32, kind="ExternalInput")
    w1T_d = nc.dram_tensor("w1T", [D, DH], F32, kind="ExternalInput")
    w2T_d = nc.dram_tensor("w2T", [D, DH], F32, kind="ExternalInput")
    h1T_d = nc.dram_tensor("h1T", [DH, DH], F32, kind="ExternalInput")
    h2T_d = nc.dram_tensor("h2T", [DH, DH], F32, kind="ExternalInput")
    w3T_d = nc.dram_tensor("w3T", [DH, D], F32, kind="ExternalInput")
    s1_d = nc.dram_tensor("s1c", [128, DH // 128], F32, kind="ExternalInput")
    s2_d = nc.dram_tensor("s2c", [128, DH // 128], F32, kind="ExternalInput")
    hs1_d = nc.dram_tensor("hs1c", [128, DH // 128], F32, kind="ExternalInput")
    hs2_d = nc.dram_tensor("hs2c", [128, DH // 128], F32, kind="ExternalInput")
    s3_d = nc.dram_tensor("s3c", [128, D // 128], F32, kind="ExternalInput")
    coff_d = nc.dram_tensor("coffm1", [128, 1], F32, kind="ExternalInput")
    out_d = nc.dram_tensor("out", [TLOC, D], F32, kind="ExternalOutput")

    kvs_d = nc.dram_tensor("kvs", [128, 2 * TLOC], BF16)
    kvg_d = nc.dram_tensor("kvg", [NCORES * 128, 2 * TLOC], BF16)
    x1_d = nc.dram_tensor("x1d", [TLOC, D], F32)
    w1b_d = nc.dram_tensor("w1b", [D, DH], BF16)
    w2b_d = nc.dram_tensor("w2b", [D, DH], BF16)
    h1b_d = nc.dram_tensor("h1b", [DH, DH], BF16)
    h2b_d = nc.dram_tensor("h2b", [DH, DH], BF16)
    w3b_d = nc.dram_tensor("w3b", [DH, D], BF16)

    tap_d = {}
    tap_shapes = {"kt": ([128, S], U16), "v": ([128, 16 * 130], U16),
                  "q": ([128, TLOC], U16), "x1": ([TLOC, D], F32),
                  "kloc": ([128, TLOC], U16), "vloc": ([128, 2048], U16),
                  "lnx": ([128, 4096], U16)}
    for t in taps:
        sh, dt = tap_shapes[t]
        tap_d[t] = nc.dram_tensor("tap_" + t, sh, dt, kind="ExternalOutput")

    pl = Plan()
    es_sem = ExitStack()
    dma_sems = []

    def mksem(name, dma=False):
        s = Sem(es_sem.enter_context(nc.semaphore(name)), name)
        if dma:
            dma_sems.append(s)
        return s

    with nc.Block() as block:
        sPE = mksem("s_pe")
        sACT = mksem("s_act")
        sDVE = mksem("s_dve")
        sGPC = mksem("s_gpc")
        pl.self_sems = {"vector": sDVE, "scalar": sACT, "gpsimd": sGPC}
        sCC = mksem("s_cc")
        sBAR = mksem("s_bar")
        sEXP = mksem("s_exp")
        sSPE = mksem("s_spe")
        sAPE = mksem("s_ape")
        sOUT = [mksem("s_out0", dma=True), mksem("s_out1", dma=True)]
        sSET = mksem("s_set", dma=True)
        sXC = [mksem("s_xc0", dma=True), mksem("s_xc1", dma=True)]
        sKSW = mksem("s_ksw", dma=True)
        sKVST = mksem("s_kvst", dma=True)
        sKV = mksem("s_kv", dma=True)
        sKV2 = mksem("s_kv2", dma=True)
        sTAB = mksem("s_tab", dma=True)
        sCVA = mksem("s_cva", dma=True)
        sCVH1 = mksem("s_cvh1", dma=True)
        sCVH2 = mksem("s_cvh2", dma=True)
        sWOV = mksem("s_wov", dma=True)
        sXOR = [mksem(f"s_xor{i}", dma=True) for i in range(8)]
        sSP = [mksem(f"s_sp{i}", dma=True) for i in range(4)]
        sX1 = [mksem("s_x10", dma=True), mksem("s_x11", dma=True)]
        sW12 = [mksem("s_w120", dma=True), mksem("s_w121", dma=True)]
        sMID = [mksem("s_mid0", dma=True), mksem("s_mid1", dma=True)]
        sTAP = mksem("s_tap", dma=True)
        sPACE = mksem("s_pace")

        def dma_snapshot(exclude=()):
            return [(s, s.n) for s in dma_sems if s.name not in exclude]

        ident = nc.alloc_sbuf_tensor("ident", [128, 128], BF16)
        invf = nc.alloc_sbuf_tensor("invf", [128, 1], F32)
        coff = nc.alloc_sbuf_tensor("coff", [128, 1], F32)
        woT_sb = nc.alloc_sbuf_tensor("woT_sb", [128, D], BF16)
        s1_sb = nc.alloc_sbuf_tensor("s1_sb", [128, DH // 128], F32)
        s2_sb = nc.alloc_sbuf_tensor("s2_sb", [128, DH // 128], F32)
        hs1_sb = nc.alloc_sbuf_tensor("hs1_sb", [128, DH // 128], F32)
        hs2_sb = nc.alloc_sbuf_tensor("hs2_sb", [128, DH // 128], F32)
        s3_sb = nc.alloc_sbuf_tensor("s3_sb", [128, D // 128], F32)
        stat = [nc.alloc_sbuf_tensor(f"stat{i}", [128, 12], F32) for i in range(2)]
        mv = [nc.alloc_sbuf_tensor(f"mv{i}", [128, 2], F32) for i in range(2)]
        rstd = [nc.alloc_sbuf_tensor(f"rstd{i}", [128, 1], F32) for i in range(2)]
        nmr = [nc.alloc_sbuf_tensor(f"nmr{i}", [128, 1], F32) for i in range(2)]
        eps_c = nc.alloc_sbuf_tensor("eps_c", [128, 1], F32)
        csh_c = nc.alloc_sbuf_tensor("csh_c", [128, 1], F32)
        sln = nc.alloc_sbuf_tensor("sln", [128, 32], F32)
        stat2 = nc.alloc_sbuf_tensor("stat2", [128, 12], F32)
        mv2 = nc.alloc_sbuf_tensor("mv2", [128, 2], F32)
        rs2 = nc.alloc_sbuf_tensor("rs2", [128, 1], F32)

        g_ms = pl.step("gpsimd", lambda g: g.memset(ident[:], 0.0), inc=sGPC)
        g_ident = pl.step("gpsimd",
                          lambda g: g.affine_select(
                              out=ident[:], in_=ident[:],
                              compare_op=ALU.not_equal, fill=1.0, base=0,
                              pattern=[[-1, 128]], channel_multiplier=1),
                          waits=((sGPC, g_ms),), inc=sGPC)
        pl.step("gpsimd", lambda g: g.memset(eps_c[:], EPS), inc=sGPC)
        g_csts = pl.step("gpsimd", lambda g: g.memset(csh_c[:], -CSHIFT),
                         inc=sGPC)
        g_invf = pl.step("gpsimd",
                         lambda g: g.iota(nmr[0][:], [[0, 1]], base=0,
                                          channel_multiplier=1,
                                          allow_small_or_imprecise_dtypes=True),
                         inc=sGPC)
        a_invf = pl.step("scalar",
                         lambda g: g.activation(invf[0:64, :], nmr[0][0:64, :],
                                                AF.Exp, bias=0.0, scale=-LN1e4_64),
                         waits=((sGPC, g_invf),), inc=sACT)

        for src, dst in ((coff_d, coff), (s1_d, s1_sb), (s2_d, s2_sb),
                         (hs1_d, hs1_sb), (hs2_d, hs2_sb), (s3_d, s3_sb)):
            pl.step("sync",
                    (lambda a, b: lambda g: g.dma_start(b[:], a.ap()))(src, dst),
                    inc=sSET, inc_k=16)
        d_setup = sSET.n

        g_cva, g_ch1, g_ch2 = (16 + 16 + 16) * 16, 64 * 16, 64 * 16

        es_kv = ExitStack()
        q_sb = es_kv.enter_context(nc.sbuf_tensor("q_sb", [128, TLOC], BF16))
        kt_sb = es_kv.enter_context(nc.sbuf_tensor("kt_sb", [128, S], BF16))
        v_sb = es_kv.enter_context(nc.sbuf_tensor("v_sb", [128, NPAN * 130], BF16))
        wq_sb = es_kv.enter_context(nc.sbuf_tensor("wq_sb", [128, 8 * HD], BF16))
        wk_sb = es_kv.enter_context(nc.sbuf_tensor("wk_sb", [128, 8 * HD], BF16))
        wv_sb = es_kv.enter_context(nc.sbuf_tensor("wv_sb", [128, 8 * HD], BF16))
        pl.step("gpsimd", lambda g: g.dma_start(woT_sb[:], woT_d.ap()),
                inc=sWOV, inc_k=16)
        for w_d, w_sb in ((wqT_d, wq_sb), (wkT_d, wk_sb), (wvT_d, wv_sb)):
            pl.step("gpsimd",
                    (lambda a, b: lambda g: g.dma_start(
                        b[:].rearrange("p (i h) -> p i h", h=HD),
                        a.ap().rearrange("(i p) h -> p i h", p=128)))(w_d, w_sb),
                    inc=sWOV, inc_k=16)
        g_wov = sWOV.n

        ev = {}
        es_p1 = ExitStack()
        xc0 = es_p1.enter_context(nc.sbuf_tensor("xc0", [128, 4 * 1024], F32))
        xc1 = es_p1.enter_context(nc.sbuf_tensor("xc1", [128, 4 * 1024], F32))
        lnx0 = es_p1.enter_context(nc.sbuf_tensor("lnx0", [128, 1024], BF16))
        lnx1 = es_p1.enter_context(nc.sbuf_tensor("lnx1", [128, 1024], BF16))
        lxt0 = es_p1.enter_context(nc.sbuf_tensor("lxt0", [128, 8 * P1CH], BF16))
        tabI = es_p1.enter_context(nc.sbuf_tensor("tabI", [128, TLOC], F32))
        tabC = es_p1.enter_context(nc.sbuf_tensor("tabC", [128, TLOC], F32))
        tabS = es_p1.enter_context(nc.sbuf_tensor("tabS", [128, TLOC], F32))
        tsc0 = es_p1.enter_context(nc.sbuf_tensor("tsc0", [128, TLOC], F32))
        tsc1 = es_p1.enter_context(nc.sbuf_tensor("tsc1", [128, TLOC], F32))
        kraw = es_p1.enter_context(nc.sbuf_tensor("kraw", [128, TLOC], F32))
        qraw = es_p1.enter_context(nc.sbuf_tensor("qraw", [128, TLOC], F32))
        ksw = es_p1.enter_context(nc.sbuf_tensor("ksw", [128, TLOC], F32))
        vt_sb = es_p1.enter_context(nc.sbuf_tensor("vt_sb", [128, TLOC], BF16))
        kloc = es_p1.enter_context(nc.sbuf_tensor("kloc", [128, TLOC], BF16))
        vloc = es_p1.enter_context(nc.sbuf_tensor("vloc", [128, 16 * 128], BF16))
        trps0 = es_p1.enter_context(nc.psum_tensor("trps0", [128, 1024], BF16))
        trps1 = es_p1.enter_context(nc.psum_tensor("trps1", [128, 1024], BF16))
        kps0 = es_p1.enter_context(nc.psum_tensor("kps0", [128, P1CH], F32))
        kps1 = es_p1.enter_context(nc.psum_tensor("kps1", [128, P1CH], F32))
        vps = es_p1.enter_context(nc.psum_tensor("vps", [128, P1CH], F32))
        qps = es_p1.enter_context(nc.psum_tensor("qps", [128, P1CH], F32))
        vtps = es_p1.enter_context(nc.psum_tensor("vtps", [128, P1CH], BF16))
        trps = [trps0, trps1]
        kps = [kps0, kps1]
        xc = [xc0, xc1]
        lnx = [lnx0, lnx1]

        tabres = {}

        def table_steps():
            g_iota = pl.step("gpsimd",
                             lambda g: g.iota(tsc0[0:64, :], [[1, TLOC]],
                                              base=0, channel_multiplier=0,
                                              allow_small_or_imprecise_dtypes=True),
                             inc=sGPC)
            yield
            pl.step("vector", lambda g: g.tensor_scalar(
                        tsc1[0:64, :], tsc0[0:64, :], coff[0:64, :], 0.0,
                        ALU.add, ALU.max),
                    waits=((sGPC, g_iota), (sSET, d_setup)), inc=sDVE)
            yield
            pl.step("vector", lambda g: g.tensor_scalar(
                        tsc0[0:64, :], tsc1[0:64, :], invf[0:64, :], None,
                        ALU.mult),
                    waits=((sACT, a_invf),), inc=sDVE)
            yield
            ang_s, ang_c = tsc0, tsc1

            def redux(src, dst_fn):
                pl.step("vector", lambda g: g.tensor_scalar(
                            tabC[0:64, :], src[0:64, :], INV2PI, MAGIC,
                            ALU.mult, ALU.add), inc=sDVE)
                yield
                pl.step("vector", lambda g: g.tensor_single_scalar(
                            tabS[0:64, :], tabC[0:64, :], MAGIC,
                            ALU.subtract), inc=sDVE)
                yield
                pl.step("vector", lambda g: g.scalar_tensor_tensor(
                            dst_fn(), tabS[0:64, :], -C1, src[0:64, :],
                            ALU.mult, ALU.add), inc=sDVE)
                yield
                pl.step("vector", lambda g: g.scalar_tensor_tensor(
                            tabC[0:64, :], tabS[0:64, :], -C2, dst_fn(),
                            ALU.mult, ALU.add), inc=sDVE)
                yield
                pl.step("vector", lambda g: g.scalar_tensor_tensor(
                            dst_fn(), tabS[0:64, :], -C3, tabC[0:64, :],
                            ALU.mult, ALU.add), inc=sDVE)
                yield

            yield from redux(ang_s, lambda: tabI[0:64, :])
            pl.step("vector", lambda g: g.tensor_single_scalar(
                        ang_c[0:64, :], ang_s[0:64, :], math.pi / 2, ALU.add),
                    inc=sDVE)
            yield
            yield from redux(ang_c, lambda: tsc0[0:64, :])
            pl.step("vector", lambda g: g.tensor_scalar(
                        tabS[0:64, :], tabI[0:64, :], PI_LO, -PI_LO,
                        ALU.min, ALU.max), inc=sDVE)
            yield
            v_tabi = pl.step("vector", lambda g: g.tensor_scalar(
                        tabC[0:64, :], tsc0[0:64, :], PI_LO, -PI_LO,
                        ALU.min, ALU.max), inc=sDVE)
            yield
            pl.step("scalar",
                    lambda g: g.activation(tabI[0:64, :], tabS[0:64, :], AF.Sin,
                                           bias=0.0, scale=1.0),
                    waits=((sDVE, v_tabi),), inc=sACT)
            yield
            a_tab = pl.step("scalar",
                            lambda g: g.activation(tsc1[0:64, :], tabC[0:64, :],
                                                   AF.Sin, bias=0.0, scale=1.0),
                            inc=sACT)
            yield
            pl.step("vector", lambda g: g.tensor_copy(tabC[0:64, :], tsc1[0:64, :]),
                    waits=((sACT, a_tab),), inc=sDVE)
            yield
            pl.step("gpsimd", lambda g: g.dma_start(tabC[64:128, :], tsc1[0:64, :]),
                    waits=((sACT, a_tab),), inc=sTAB, inc_k=16)
            yield
            d_ts = pl.step("gpsimd", lambda g: g.dma_start(tabS[64:128, :],
                                                           tabI[0:64, :]),
                           inc=sTAB, inc_k=16)
            yield
            v_tabs = pl.step("vector",
                             lambda g: g.tensor_scalar_mul(
                                 tabS[0:64, :], tabI[0:64, :], -1.0),
                             waits=((sTAB, d_ts),), inc=sDVE)

            tabres['v_tabs'] = v_tabs
            yield

        tgen = table_steps()
        _tdone = object()

        def pump_tables(n):
            for _ in range(n):
                if next(tgen, _tdone) is _tdone:
                    break

        for c in range(NP1):
            slot = c % 2
            w = []
            if c >= 2:
                w.append((sPE, ev[("proj", c - 2)]))
            d_x = pl.step("sync", (lambda cc_, sl: lambda g: g.dma_start(
                        xc[sl][:].rearrange("p (t d) -> p t d", d=D),
                        x_d.ap()[cc_ * P1CH:(cc_ + 1) * P1CH, :]
                        .rearrange("(t p) d -> p t d", p=128)))(c, slot),
                    waits=w, inc=sXC[slot], inc_k=16)
            for tt in range(4):
                par = tt % 2
                xcol = 1024 * tt
                pl.step("vector", (lambda sl, co, pp: lambda g: g.bn_stats(
                            stat[pp][:, 0:6], xc[sl][:, co:co + 512]))(slot, xcol, par),
                        waits=((sXC[slot], d_x),), inc=sDVE)
                pl.step("vector", (lambda sl, co, pp: lambda g: g.bn_stats(
                            stat[pp][:, 6:12], xc[sl][:, co + 512:co + 1024]))(slot, xcol, par),
                        inc=sDVE)
                v_st = pl.step("vector", (lambda pp: lambda g: g.bn_aggr(
                            mv[pp][:], stat[pp][:]))(par), inc=sDVE)
                a_ln = pl.step("scalar", (lambda pp: lambda g: g.activation(
                            rstd[pp][:], mv[pp][:, 1:2], AF.Ln, bias=eps_c[:],
                            scale=1.0))(par),
                        waits=((sDVE, v_st), (sGPC, g_csts)), inc=sACT)
                a_ex = pl.step("scalar", (lambda pp: lambda g: g.activation(
                            rstd[pp][:], rstd[pp][:], AF.Exp, bias=0.0,
                            scale=-0.5))(par),
                        waits=((sACT, a_ln),), inc=sACT)
                v_nm = pl.step("vector", (lambda pp: lambda g: g.scalar_tensor_tensor(
                            nmr[pp][:], mv[pp][:, 0:1], -1.0, rstd[pp][:],
                            ALU.mult, ALU.mult))(par),
                        waits=((sACT, a_ex),), inc=sDVE)
                w = [(sDVE, v_nm)]
                prev_tr = ev.get(("tr1", c, tt - 2)) or ev.get(("tr1", c - 1, tt + 2))
                if prev_tr:
                    w.append((sPE, prev_tr))
                a_nrm = pl.step("scalar", (lambda sl, co, pp: lambda g: g.activation(
                            lnx[pp][:], xc[sl][:, co:co + 1024], AF.Identity,
                            bias=nmr[pp][:], scale=rstd[pp][:]))(slot, xcol, par),
                        waits=w, inc=sACT)
                prev_ev = ev.get(("trv1", c, tt - 2)) or ev.get(("trv1", c - 1, tt + 2))
                for i in range(8):
                    w = [(sACT, a_nrm), (sGPC, g_ident)]
                    if prev_ev:
                        w.append((sDVE, prev_ev))
                    pl.step("tensor", (lambda pp, ii: lambda g: g.transpose(
                                trps[pp][:, 128 * ii:128 * ii + 128],
                                lnx[pp][:, 128 * ii:128 * ii + 128],
                                ident[:]))(par, i),
                            waits=w, inc=(sPE if i == 7 else None))
                ev[("tr1", c, tt)] = sPE.n
                wev = [(sPE, ev[("tr1", c, tt)])]
                if tt == 0 and c >= 1:
                    wev.append((sPE, ev[("proj", c - 1)]))
                ev[("trv1", c, tt)] = pl.step(
                    "vector", (lambda pp, t_: lambda g: g.tensor_copy(
                        lxt0[:].rearrange("p (i t) -> p i t", t=P1CH)[:, :, 128 * t_:128 * t_ + 128],
                        trps[pp][:].rearrange("p (i t) -> p i t", t=128)))(par, tt),
                    waits=wev, inc=sDVE)
                if c >= 1:
                    pump_tables(2)
            v_lxt = ev[("trv1", c, 3)]

            for name, wsb, ps in (("k", wk_sb, kps[c % 2]),
                                  ("v", wv_sb, vps),
                                  ("q", wq_sb, qps)):
                w = [(sDVE, v_lxt), (sWOV, g_wov)]
                if name == "k" and c >= 2:
                    w.append((sACT, ev[("kev", c - 2)]))
                if name in ("v", "q") and c >= 1:
                    w.append((sACT, ev[(name + "ev", c - 1)]))
                for i in range(8):
                    pl.step("tensor", (lambda wsb_, ps_, ii: lambda g: g.matmul(
                                ps_[:], wsb_[:, 128 * ii:128 * ii + 128],
                                lxt0[:, P1CH * ii:P1CH * (ii + 1)],
                                start=(ii == 0), stop=(ii == 7)))(wsb, ps, i),
                            waits=(w if i == 0 else ()),
                            inc=(sPE if i == 7 else None))
                ev[(name + "pe", c)] = sPE.n
            ev[("proj", c)] = sPE.n

            kcol = P1CH * c
            ev[("kev", c)] = pl.step(
                "scalar", (lambda ps_, co: lambda g: g.activation(
                    kraw[:, co:co + P1CH], ps_[:], AF.Copy, bias=0.0,
                    scale=1.0))(kps[c % 2], kcol),
                waits=((sPE, ev[("kpe", c)]),), inc=sACT)
            ev[("vev", c)] = pl.step(
                "scalar", (lambda co: lambda g: g.activation(
                    vt_sb[:, co:co + P1CH], vps[:], AF.Copy, bias=0.0,
                    scale=1.0))(kcol),
                waits=((sPE, ev[("vpe", c)]),), inc=sACT)
            ev[("qev", c)] = pl.step(
                "scalar", (lambda co: lambda g: g.activation(
                    qraw[:, co:co + P1CH], qps[:], AF.Copy, bias=0.0,
                    scale=1.0))(kcol),
                waits=((sPE, ev[("qpe", c)]),), inc=sACT)

            for k in range(4):
                w = [(sACT, ev[("vev", c)])]
                prev = ev.get(("vtv", c, k - 1)) or ev.get(("vtv", c - 1, 3))
                if prev:
                    w.append((sDVE, prev))
                p_vt = pl.step("tensor", (lambda co, kk_: lambda g: g.transpose(
                            vtps[:, 128 * kk_:128 * kk_ + 128],
                            vt_sb[:, co + 128 * kk_:co + 128 * kk_ + 128],
                            ident[:]))(kcol, k),
                        waits=w, inc=sPE)
                ev[("vtv", c, k)] = pl.step(
                    "vector", (lambda cc_, kk_: lambda g: g.tensor_copy(
                        vloc[:, 128 * (4 * cc_ + kk_):128 * (4 * cc_ + kk_) + 128],
                        vtps[:, 128 * kk_:128 * kk_ + 128]))(c, k),
                    waits=((sPE, p_vt),), inc=sDVE)
        v_vloc = ev[("vtv", NP1 - 1, 3)]
        pump_tables(99)

        pl.step("sync", lambda g: g.dma_start(ksw[0:64, :], kraw[64:128, :]),
                waits=((sACT, ev[("kev", NP1 - 1)]),), inc=sKSW, inc_k=16)
        d_ksw = pl.step("sync",
                        lambda g: g.dma_start(ksw[64:128, :], kraw[0:64, :]),
                        inc=sKSW, inc_k=16)
        pl.step("vector", lambda g: g.tensor_mul(tsc0[:], kraw[:], tabC[:]),
                waits=((sKSW, d_ksw), (sDVE, tabres['v_tabs'])), inc=sDVE)
        pl.step("vector", lambda g: g.tensor_mul(tsc1[:], ksw[:], tabS[:]),
                inc=sDVE)
        v_krot = pl.step("vector",
                         lambda g: g.tensor_add(kloc[:], tsc0[:], tsc1[:]),
                         inc=sDVE)
        pl.step("sync", lambda g: g.dma_start(
                    kvs_d.ap()[:, 0:TLOC], kloc[:]),
                waits=((sDVE, v_krot),), inc=sKVST, inc_k=16)
        d_kvst = pl.step("sync", lambda g: g.dma_start(
                    kvs_d.ap()[:, TLOC:2 * TLOC], vloc[:]),
                waits=((sDVE, v_vloc),), inc=sKVST, inc_k=16)
        cc_done = pl.step("gpsimd", lambda g: g.collective_compute(
                    "AllGather", ALU.bypass,
                    replica_groups=[list(range(NCORES))],
                    ins=[kvs_d.ap().opt()], outs=[kvg_d.ap().opt()]),
                waits=((sKVST, d_kvst),), inc=sCC)

        pl.step("sync", lambda g: g.dma_start(ksw[0:64, :], qraw[64:128, :]),
                waits=((sACT, ev[("qev", NP1 - 1)]), (sDVE, v_krot),
                       (sKVST, d_kvst)),
                inc=sKSW, inc_k=16, force_waits=True)
        d_qsw = pl.step("sync",
                        lambda g: g.dma_start(ksw[64:128, :], qraw[0:64, :]),
                        inc=sKSW, inc_k=16)
        pl.step("vector", lambda g: g.tensor_mul(tsc0[:], qraw[:], tabC[:]),
                waits=((sKSW, d_qsw),), inc=sDVE)
        pl.step("vector", lambda g: g.tensor_mul(tsc1[:], ksw[:], tabS[:]),
                inc=sDVE)
        v_qrot = pl.step("vector",
                         lambda g: g.tensor_add(q_sb[:], tsc0[:], tsc1[:]),
                         inc=sDVE)

        if "q" in taps:
            pl.step("sync", lambda g: g.dma_start(
                        tap_d["q"].ap(), q_sb[:].bitcast(U16)),
                    waits=((sDVE, v_qrot),), inc=sTAP, inc_k=16)
        if "kloc" in taps:
            pl.step("sync", lambda g: g.dma_start(
                        tap_d["kloc"].ap(), kloc[:].bitcast(U16)),
                    waits=((sDVE, v_krot),), inc=sTAP, inc_k=16)
        if "vloc" in taps:
            pl.step("sync", lambda g: g.dma_start(
                        tap_d["vloc"].ap(), vloc[:].bitcast(U16)),
                    waits=((sDVE, v_vloc),), inc=sTAP, inc_k=16)
        if "lnx" in taps:
            pl.step("sync", lambda g: g.dma_start(
                        tap_d["lnx"].ap(), lxt0[:].bitcast(U16)),
                    waits=((sDVE, ev[("trv1", 3, 3)]),), inc=sTAP, inc_k=16)

        es_p1.close()
        pl.barrier(sBAR, extra_waits=dma_snapshot())

        d_kt = pl.step("sync", lambda g: g.dma_start(
                    kt_sb[:].rearrange("h (r t) -> h r t", t=TLOC),
                    kvg_d.ap()[:, 0:TLOC].rearrange("(r h) t -> h r t", h=128)),
                waits=((sCC, cc_done),), inc=sKV, inc_k=16)
        for rr in range(NCORES):
            pl.step("scalar", (lambda r_: lambda g: g.dma_start(
                        v_sb[:, 130 * 16 * r_:130 * 16 * (r_ + 1)]
                        .rearrange("p (k c) -> p k c", c=130)[:, :, 0:128],
                        kvg_d.ap()[128 * r_:128 * (r_ + 1), TLOC:2 * TLOC]
                        .rearrange("p (k h) -> p k h", h=128)))(rr),
                    waits=(((sCC, cc_done),) if rr == 0 else ()),
                    inc=sKV2, inc_k=16)
        d_kv = sKV2.n
        g_ones = pl.step("gpsimd", lambda g: g.memset(
                    v_sb[:].rearrange("p (k c) -> p k c", c=130)[:, :, 128:130],
                    1.0),
                waits=((sKV2, d_kv),), inc=sGPC)
        piece = 0

        def conv_gate():
            if piece < 48:
                return (sAPE, piece * 4)
            if piece < 80:
                return (sAPE, 194 + (piece - 48) * 2)
            if piece < 112:
                return (sPACE, (piece - 79))
            return (sPACE, 33 + (piece - 112) * 3 // 8)

        for csrc, cdst, csem, nrow in ((w1T_d, w1b_d, sCVA, 64),
                                       (w2T_d, w2b_d, sCVA, 64),
                                       (w3T_d, w3b_d, sCVA, 256),
                                       (h1T_d, h1b_d, sCVH1, 64),
                                       (h2T_d, h2b_d, sCVH2, 64)):
            nr = csrc.shape[0]
            for r0 in range(0, nr, nrow):
                pl.step("gpsimd",
                        (lambda a, b, r_, n_: lambda g: g.dma_start(
                            b.ap()[r_:r_ + n_, :], a.ap()[r_:r_ + n_, :]))(
                                csrc, cdst, r0, nrow),
                        waits=(conv_gate(),), inc=csem, inc_k=16)
                piece += 1
        assert (sCVA.n, sCVH1.n, sCVH2.n) == (g_cva, g_ch1, g_ch2)

        if "kt" in taps:
            pl.step("sync", lambda g: g.dma_start(
                        tap_d["kt"].ap(), kt_sb[:].bitcast(U16)),
                    waits=((sKV, d_kt),), inc=sTAP, inc_k=16)
        if "v" in taps:
            pl.step("sync", lambda g: g.dma_start(
                        tap_d["v"].ap(), v_sb[:, 0:16 * 130].bitcast(U16)),
                    waits=((sGPC, g_ones),), inc=sTAP, inc_k=16)

        NPR = NPAN // 2
        x1_spill = [0, 0, 0, 0]
        es_at = ExitStack()
        a0 = es_at.enter_context(nc.sbuf_tensor("a0", [128, 1024], BF16))
        a1 = es_at.enter_context(nc.sbuf_tensor("a1", [128, 1024], BF16))
        a2 = es_at.enter_context(nc.sbuf_tensor("a2", [128, 1024], BF16))
        a3 = es_at.enter_context(nc.sbuf_tensor("a3", [128, 1024], BF16))
        o_sb = es_at.enter_context(nc.sbuf_tensor("o_sb", [128, 4 * 128], BF16))
        ot_sb = es_at.enter_context(nc.sbuf_tensor("ot_sb", [128, 512], BF16))
        recl = es_at.enter_context(nc.sbuf_tensor("recl", [128, 4], F32))
        xorb = es_at.enter_context(nc.sbuf_tensor("xorb", [128, 8 * 1024], F32))
        tA0 = es_at.enter_context(nc.sbuf_tensor("tA0", [128, 1024], F32))
        tA1 = es_at.enter_context(nc.sbuf_tensor("tA1", [128, 1024], F32))
        x1h0 = es_at.enter_context(nc.sbuf_tensor("x1h0", [128, 4 * 1024], F32))
        x1h1 = es_at.enter_context(nc.sbuf_tensor("x1h1", [128, 4 * 1024], F32))
        sps0 = es_at.enter_context(nc.psum_tensor("sps0", [128, 1024], F32))
        sps1 = es_at.enter_context(nc.psum_tensor("sps1", [128, 1024], F32))
        ops_t = [es_at.enter_context(nc.psum_tensor(f"ops{i}", [128, 512], F32))
                 for i in range(4)]
        a_sb = [a0, a1, a2, a3]
        sps = [sps0, sps1]
        otps = ops_t[0][:, 0:256].bitcast(BF16)
        tA = [tA0, tA1]
        x1h = [x1h0, x1h1]
        aev = {}

        def emit_ln2_stats(b):
            prev_aln = None
            for s in range(4):
                t = 4 * b + s
                sl = x1h[b % 2][:, 1024 * s:1024 * (s + 1)]
                pl.step("vector", (lambda sl_: lambda g: g.bn_stats(
                            stat2[:, 0:6], sl_[:, 0:512]))(sl),
                        waits=((sDVE, aev[("x1", b, s)]),), inc=sDVE)
                pl.step("vector", (lambda sl_: lambda g: g.bn_stats(
                            stat2[:, 6:12], sl_[:, 512:1024]))(sl), inc=sDVE)
                w = [(sACT, prev_aln)] if prev_aln else []
                v_st = pl.step("vector", lambda g: g.bn_aggr(mv2[:], stat2[:]),
                               waits=w, inc=sDVE)
                a_ln = pl.step("scalar", lambda g: g.activation(
                            rs2[:], mv2[:, 1:2], AF.Ln, bias=eps_c[:], scale=1.0),
                        waits=((sDVE, v_st),), inc=sACT)
                prev_aln = a_ln
                a_ex = pl.step("scalar", (lambda t_: lambda g: g.activation(
                            sln[:, 2 * t_ + 1:2 * t_ + 2], rs2[:], AF.Exp,
                            bias=0.0, scale=-0.5))(t),
                        waits=((sACT, a_ln),), inc=sACT)
                ev[("lnst", t)] = pl.step(
                    "vector", (lambda t_: lambda g: g.scalar_tensor_tensor(
                        sln[:, 2 * t_:2 * t_ + 1], mv2[:, 0:1], -1.0,
                        sln[:, 2 * t_ + 1:2 * t_ + 2],
                        ALU.mult, ALU.mult))(t),
                    waits=((sACT, a_ex),), inc=sDVE)

        def s_mm(b, i):
            pr = NPR * b + i
            w = [(sKV, d_kt), (sDVE, v_qrot)]
            if pr >= 2:
                w.append((sEXP, pr - 1))
            if i in (0, 1) and b >= 1:
                w.append((sACT, aev[("aff", b - 1, 3 if pr % 2 == 0 else 2)]))
            for half in range(2):
                pl.step("tensor", (lambda ii, hh, par, bb: lambda g: g.matmul(
                            sps[par][:, 512 * hh:512 * hh + 512],
                            kt_sb[:, 128 * (2 * ii + hh):128 * (2 * ii + hh) + 128],
                            q_sb[:, QBLK * bb:QBLK * bb + 512],
                            start=True, stop=True))(i, half, pr % 2, b),
                        waits=(w if half == 0 else ()),
                        inc=(sSPE if half == 1 else None))

        def exp_step(b, i):
            pr = NPR * b + i
            w = [(sSPE, pr + 1), (sGPC, g_csts)]
            if pr >= 4:
                w.append((sAPE, pr - 3))
            pl.step("scalar", (lambda par, spar: lambda g: g.activation(
                        a_sb[par][:], sps[spar][:], AF.Exp,
                        bias=csh_c[:], scale=ISQ))(pr % 4, pr % 2),
                    waits=w, inc=sEXP)

        def av_mm(b, i):
            pr = NPR * b + i
            w = [(sEXP, pr + 1), (sGPC, g_ones)]
            if i == 0 and b >= 1:
                w.append((sDVE, aev[("otv_last", b - 1)]))
                w.append((sACT, aev[("oev_last", b - 1)]))
            for s in range(4):
                for j in range(2):
                    p = 2 * i + j
                    pl.step("tensor", (lambda par, pp, jj, s_: lambda g: g.matmul(
                                ops_t[s_][:, 0:129],
                                a_sb[par][:, 512 * jj + 128 * s_:512 * jj + 128 * s_ + 128],
                                v_sb[:, 130 * pp:130 * pp + 129],
                                start=(pp == 0), stop=(pp == NPAN - 1)))(pr % 4, p, j, s),
                            waits=(w if (s == 0 and j == 0) else ()),
                            inc=(sAPE if (s == 3 and j == 1) else None))

        def xor_slot(b, s):
            return xorb[:, 1024 * ((b % 2) * 4 + s):1024 * ((b % 2) * 4 + s) + 1024]

        def emit_xor_loads(b):
            for s in range(4):
                w = []
                if b >= 2:
                    w.append((sDVE, aev[("x1", b - 2, s)]))
                aev[("xord", b, s)] = pl.step(
                    "sync", (lambda bb, s_: lambda g: g.dma_start(
                        xor_slot(bb, s_),
                        x_d.ap()[QBLK * bb + 128 * s_:QBLK * bb + 128 * s_ + 128, :]))(b, s),
                    waits=w, inc=sXOR[(b % 2) * 4 + s], inc_k=16)

        def emit_res_add(b, s):
            w = [(sACT, aev[("aff", b, s)]),
                 (sXOR[(b % 2) * 4 + s], aev[("xord", b, s)])]
            if s == 0 and b >= 2:
                w.append((sSP[b - 2], x1_spill[b - 2]))
                w.append((sDVE, ev[("lnst", 4 * (b - 2) + 3)]))
            aev[("x1", b, s)] = pl.step(
                "vector", (lambda s_, bb: lambda g: g.tensor_add(
                    x1h[bb % 2][:, 1024 * s_:1024 * (s_ + 1)], tA[s_ % 2][:],
                    xor_slot(bb, s_)))(s, b),
                waits=w, inc=sDVE)

        for b in range(NQB):
            emit_xor_loads(b)
            s_mm(b, 0)
            s_mm(b, 1)
            for i in range(NPR):
                exp_step(b, i)
                av_mm(b, i)
                if i + 2 < NPR:
                    s_mm(b, i + 2)
            ape_blk = sAPE.n

            for s in range(4):
                w = [(sAPE, ape_blk)]
                if b >= 1:
                    w.append((sACT, aev[("oev_last", b - 1)]))
                aev[("rl", b, s)] = pl.step(
                    "vector", (lambda s_: lambda g: g.reciprocal(
                        recl[:, s_:s_ + 1], ops_t[s_][:, 128:129]))(s),
                    waits=w, inc=sDVE)
                w = [(sDVE, aev[("rl", b, s)])]
                if b >= 1:
                    w.append((sPE, aev[("otr_last", b - 1)]))
                aev[("oev", b, s)] = pl.step(
                    "scalar", (lambda s_: lambda g: g.activation(
                        o_sb[:, 128 * s_:128 * s_ + 128],
                        ops_t[s_][:, 0:128], AF.Copy,
                        bias=0.0, scale=recl[:, s_:s_ + 1]))(s),
                    waits=w, inc=sACT)
                w = [(sACT, aev[("oev", b, s)])]
                if b >= 1:
                    w.append((sDVE, aev[("otv_last", b - 1)]))
                p_ot = pl.step("tensor", (lambda s_: lambda g: g.transpose(
                            otps[:, 128 * s_:128 * s_ + 128],
                            o_sb[:, 128 * s_:128 * s_ + 128], ident[:]))(s),
                        waits=w, inc=sPE)
                w = [(sPE, p_ot)]
                if b >= 1:
                    w.append((sPE, aev[("ype", b - 1, 3)]))
                aev[("otv", b, s)] = pl.step(
                    "vector", (lambda s_: lambda g: g.tensor_copy(
                        ot_sb[:, 128 * s_:128 * s_ + 128],
                        otps[:, 128 * s_:128 * s_ + 128]))(s),
                    waits=w, inc=sDVE)
            aev[("oev_last", b)] = aev[("oev", b, 3)]
            aev[("otv_last", b)] = aev[("otv", b, 3)]
            aev[("otr_last", b)] = sPE.n

            for s in range(4):
                par = s % 2
                w = [(sDVE, aev[("otv", b, s)]),
                     (sEXP, (b + 1) * NPR), (sWOV, g_wov)]
                if s >= 2:
                    w.append((sACT, aev[("aff", b, s - 2)]))
                for half in range(2):
                    pl.step("tensor", (lambda s_, pp, hh_: lambda g: g.matmul(
                                sps[pp][:, 512 * hh_:512 * hh_ + 512],
                                ot_sb[:, 128 * s_:128 * s_ + 128],
                                woT_sb[:, 512 * hh_:512 * hh_ + 512],
                                start=True, stop=True))(s, par, half),
                            waits=(w if half == 0 else ()),
                            inc=(sPE if half == 1 else None))
                aev[("ype", b, s)] = sPE.n
                pl.step("vector", (lambda pp: lambda g: g.bn_stats(
                            stat[pp][:, 0:6], sps[pp][:, 0:512]))(par),
                        waits=((sPE, aev[("ype", b, s)]),), inc=sDVE)
                pl.step("vector", (lambda pp: lambda g: g.bn_stats(
                            stat[pp][:, 6:12], sps[pp][:, 512:1024]))(par),
                        inc=sDVE)
                v_st = pl.step("vector", (lambda pp: lambda g: g.bn_aggr(
                            mv[pp][:], stat[pp][:]))(par), inc=sDVE)
                a_ln = pl.step("scalar", (lambda pp: lambda g: g.activation(
                            rstd[pp][:], mv[pp][:, 1:2], AF.Ln, bias=eps_c[:],
                            scale=1.0))(par),
                        waits=((sDVE, v_st), (sGPC, g_csts)), inc=sACT)
                a_ex = pl.step("scalar", (lambda pp: lambda g: g.activation(
                            rstd[pp][:], rstd[pp][:], AF.Exp, bias=0.0,
                            scale=-0.5))(par),
                        waits=((sACT, a_ln),), inc=sACT)
                v_nm = pl.step("vector", (lambda pp: lambda g: g.scalar_tensor_tensor(
                            nmr[pp][:], mv[pp][:, 0:1], -1.0, rstd[pp][:],
                            ALU.mult, ALU.mult))(par),
                        waits=((sACT, a_ex),), inc=sDVE)
                w = [(sDVE, v_nm)]
                if s >= 2:
                    w.append((sDVE, aev[("x1", b, s - 2)]))
                aev[("aff", b, s)] = pl.step(
                    "scalar", (lambda s_: lambda g: g.activation(
                        tA[s_ % 2][:], sps[s_ % 2][:], AF.Identity,
                        bias=nmr[s_ % 2][:], scale=rstd[s_ % 2][:]))(s),
                    waits=w, inc=sACT)
                if s >= 1:
                    emit_res_add(b, s - 1)

            if b >= 1:
                emit_ln2_stats(b - 1)
            emit_res_add(b, 3)
            x1_spill[b] = pl.step(
                "sync", (lambda bb: lambda g: g.dma_start(
                    x1_d.ap()[QBLK * bb:QBLK * (bb + 1), :]
                    .rearrange("(t p) d -> p t d", p=128),
                    x1h[bb % 2][:].rearrange("p (t d) -> p t d", d=D)))(b),
                waits=((sDVE, aev[("x1", b, 3)]),), inc=sSP[b], inc_k=16)

        if "x1" in taps:
            pl.step("sync", lambda g: g.dma_start(
                        tap_d["x1"].ap(), x1_d.ap()),
                    waits=tuple((sSP[b], 16) for b in range(4)),
                    inc=sTAP, inc_k=16)

        es_at.close()
        es_kv.close()

        es_f = ExitStack()
        Hbuf = es_f.enter_context(nc.sbuf_tensor("Hbuf", [128, 32 * FCH], BF16))
        G1buf = es_f.enter_context(nc.sbuf_tensor("G1buf", [128, 32 * FCH], BF16))
        mida = es_f.enter_context(nc.sbuf_tensor("mida", [128, 2 * 32 * 256], BF16))
        x1a0 = es_f.enter_context(nc.sbuf_tensor("x1a0", [128, 1024], F32))
        x1a1 = es_f.enter_context(nc.sbuf_tensor("x1a1", [128, 1024], F32))
        shA = es_f.enter_context(nc.sbuf_tensor("shA", [128, 1024], F32))
        x2b0 = es_f.enter_context(nc.sbuf_tensor("x2b0", [128, 1024], F32))
        x2b1 = es_f.enter_context(nc.sbuf_tensor("x2b1", [128, 1024], F32))
        t1sil0 = es_f.enter_context(nc.sbuf_tensor("t1sil0", [128, FCH], BF16))
        t1sil1 = es_f.enter_context(nc.sbuf_tensor("t1sil1", [128, FCH], BF16))
        lxt2 = es_f.enter_context(nc.sbuf_tensor("lxt2", [128, 8 * FCH], BF16))
        pgA = es_f.enter_context(nc.psum_tensor("pgA", [128, 1024], F32))
        pgB = es_f.enter_context(nc.psum_tensor("pgB", [128, 1024], F32))
        pgC = es_f.enter_context(nc.psum_tensor("pgC", [128, 1024], F32))
        pgD = es_f.enter_context(nc.psum_tensor("pgD", [128, 1024], F32))
        x1a = [x1a0, x1a1]
        x1b = x1a
        lnx2 = [shA[:, 0:512].bitcast(BF16), shA[:, 512:1024].bitcast(BF16)]
        ft3 = shA[:]
        YT = G1buf[:, 0:8 * FCH]
        t1sil = [t1sil0, t1sil1]
        x2b = [x2b0, x2b1]
        pg = [pgA, pgB, pgC, pgD]
        trf = [pgA[:, 0:512].bitcast(BF16), pgB[:, 0:512].bitcast(BF16)]
        ytk = [pgC[:, 0:512].bitcast(BF16), pgD[:, 0:512].bitcast(BF16)]


        def w12_slab(mat, grp):
            base = ((grp % 2) * 2 + mat) * 8 * 256
            return mida[:, base:base + 8 * 256]

        def mid_slot(pair):
            base = (pair % 2) * 32 * 256
            return mida[:, base:base + 32 * 256]

        apbuf = {(0, s): [x1a0, x1a1, x2b0, x2b1][s % 4] for s in range(8)}
        apbuf.update({(1, s): [x1a0, x1a1][s % 2] for s in range(8)})

        def ap_load(c, s, prev, extra=()):
            w = [(sSP[(FCH * c + 128 * s) // QBLK], 16)] + list(extra)
            if prev:
                w.append((sACT, prev))
            ev[("x1d", c, s)] = pl.step(
                "sync", (lambda r0, bb: lambda g: g.dma_start(
                    bb[:], x1_d.ap()[r0:r0 + 128, :]))(FCH * c + 128 * s,
                                                       apbuf[(c, s)]),
                waits=w, inc=sX1[s % 2], inc_k=16)

        def ap_affine(c, s):
            par = s % 2
            t = 8 * c + s
            w = [(sX1[par], ev[("x1d", c, s)]), (sDVE, ev[("lnst", t)])]
            prev_t = ev.get(("tr2", c, s - 2)) or ev.get(("tr2", c - 1, s + 6))
            if prev_t:
                w.append((sPE, prev_t))
            ev[("nrm2", c, s)] = pl.step(
                "scalar", (lambda pp, bb, b0: lambda g: g.activation(
                    lnx2[pp], bb[:], AF.Identity,
                    bias=sln[:, b0:b0 + 1], scale=sln[:, b0 + 1:b0 + 2]))(
                        par, apbuf[(c, s)], 2 * t),
                waits=w, inc=sACT)

        def ap_transpose(c, s, trdst, trw=()):
            par = s % 2
            prev_tv = ev.get(("trv2", c, s - 2)) or ev.get(("trv2", c - 1, s + 6))
            for i in range(8):
                w = [(sACT, ev[("nrm2", c, s)])]
                if i == 0:
                    w += list(trw)
                    if prev_tv:
                        w.append((sDVE, prev_tv))
                pl.step("tensor", (lambda dd, pp, ii: lambda g: g.transpose(
                            dd[:, 128 * ii:128 * ii + 128],
                            lnx2[pp][:, 128 * ii:128 * ii + 128],
                            ident[:]))(trdst, par, i),
                        waits=w, inc=(sPE if i == 7 else None))
            ev[("tr2", c, s)] = sPE.n
            ev[("trv2", c, s)] = pl.step(
                "vector", (lambda dd, s_: lambda g: g.tensor_copy(
                    lxt2[:].rearrange("p (i t) -> p i t", t=FCH)[:, :, 128 * s_:128 * s_ + 128],
                    dd.rearrange("p (i t) -> p i t", t=128)))(trdst, s),
                waits=((sPE, ev[("tr2", c, s)]),), inc=sDVE)

        ap_load(0, 0, None)
        ap_load(0, 1, None)
        for s in range(8):
            trw = [(sACT, aev[("aff", 3, 3)])] if s < 2 else ()
            ap_affine(0, s)
            ap_transpose(0, s, trf[s % 2], trw)
            if s + 2 < 8:
                ap_load(0, s + 2, ev[("nrm2", 0, s)])
        emit_ln2_stats(3)

        pl.barrier(sBAR, extra_waits=dma_snapshot(
            exclude=("s_cva", "s_cvh1", "s_cvh2")) + [(sGPC, sGPC.n)],
            engines=("tensor", "vector", "scalar", "sync"))

        for c in range(NFCH):
            tc0 = FCH * c
            v_lxt2 = ev[("trv2", c, 7)]

            def emit_w12_load(grp):
                w = []
                if grp >= 2:
                    w.append((sPE, ev[("bpe", c, 2 * (grp - 2) + 1)]))
                elif c >= 1:
                    w.append((sPE, ev[("epe", c - 1, 7)]))
                if grp == 0 and c == 0:
                    w.append((sCVA, g_cva))
                for mat, wsrc in ((0, w1b_d), (1, w2b_d)):
                    for i in range(8):
                        pl.step("sync", (lambda ws, gg, mm, ii: lambda g: g.dma_start(
                                    w12_slab(mm, gg)[:, 256 * ii:256 * (ii + 1)],
                                    ws.ap()[128 * ii:128 * (ii + 1),
                                            256 * gg:256 * (gg + 1)]))(wsrc, grp, mat, i),
                                waits=(w if (mat == 0 and i == 0) else ()),
                                inc=sW12[grp % 2], inc_k=16)
                ev[("w12d", c, grp)] = sW12[grp % 2].n

            emit_w12_load(0)
            emit_w12_load(1)
            for j in range(32):
                grp = j // 2
                if j % 2 == 0 and grp >= 1 and grp + 1 < 16:
                    emit_w12_load(grp + 1)
                ps1, ps2 = pg[2 * (j % 2)], pg[2 * (j % 2) + 1]
                w = [(sDVE, v_lxt2), (sW12[grp % 2], ev[("w12d", c, grp)])]
                if j >= 2:
                    w.append((sACT, ev[("bsil", c, j - 2)]))
                    w.append((sDVE, ev[("bh", c, j - 2)]))
                elif c == 0:
                    w.append((sACT, aev[("aff", 3, 3)]))
                else:
                    w.append((sACT, ev[("aff3", c - 1, 7)]))
                    w.append((sDVE, ev[("fst", c - 1, 7)]))
                    if j == 0:
                        w.append((sPE, ev[("epe", c - 1, 7)]))
                for mat, ps in ((0, ps1), (1, ps2)):
                    for i in range(8):
                        for half in range(2):
                            pl.step("tensor",
                                    (lambda mm, jj, ii, hf, ps_: lambda g: g.matmul(
                                        ps_[:, 512 * hf:512 * hf + 512],
                                        w12_slab(mm, jj // 2)[:, 256 * ii + 128 * (jj % 2):256 * ii + 128 * (jj % 2) + 128],
                                        lxt2[:, FCH * ii + 512 * hf:FCH * ii + 512 * hf + 512],
                                        start=(ii == 0), stop=(ii == 7)))(mat, j, i, half, ps),
                                    waits=(w if (mat == 0 and i == 0 and half == 0) else ()),
                                    inc=(sPE if (mat == 1 and i == 7 and half == 1) else None))
                ev[("bpe", c, j)] = sPE.n
                if c == 0:
                    pl.step("tensor", lambda g: g.nop(), inc=sPACE)
                ev[("bsil", c, j)] = pl.step(
                    "scalar", (lambda jj, ps_: lambda g: g.activation(
                        t1sil[jj % 2][:], ps_[:], AF.Silu, bias=0.0,
                        scale=s1_sb[:, jj:jj + 1]))(j, ps1),
                    waits=((sPE, ev[("bpe", c, j)]),), inc=sACT)
                ev[("bh", c, j)] = pl.step(
                    "vector", (lambda jj, ps_: lambda g: g.scalar_tensor_tensor(
                        Hbuf[:, FCH * jj:FCH * (jj + 1)], ps_[:],
                        s2_sb[:, jj:jj + 1], t1sil[jj % 2][:],
                        ALU.mult, ALU.mult))(j, ps2),
                    waits=((sACT, ev[("bsil", c, j)]),), inc=sDVE)
            ev[("w12pe", c)] = ev[("bpe", c, 31)]
            v_H = ev[("bh", c, 31)]

            def middle(layer, wsrc, src_buf, dst_buf, scale_sb, conv_sem,
                       conv_n, inject=None):
                def emit_mid_load(pair):
                    w = [(conv_sem, conv_n)]
                    if pair >= 2:
                        w.append((sPE, ev[("mpe", layer, c, 2 * (pair - 2) + 1)]))
                    elif layer == 1:
                        w.append((sPE, ev[("mpe", 0, c, 2 * (pair + 14) + 1)]))
                    elif pair == 0:
                        w.append((sPE, ev[("bpe", c, 31)]))
                    elif c >= 1:
                        w.append((sPE, ev[("epe", c - 1, 7)]))
                    for hh in range(32):
                        pl.step("sync", (lambda pp, h_: lambda g: g.dma_start(
                                    mid_slot(pp)[:, 256 * h_:256 * (h_ + 1)],
                                    wsrc.ap()[128 * h_:128 * (h_ + 1),
                                              256 * pp:256 * (pp + 1)]))(pair, hh),
                                waits=(w if hh == 0 else ()),
                                inc=sMID[pair % 2], inc_k=16)
                    ev[("midd", layer, c, pair)] = sMID[pair % 2].n

                emit_mid_load(0)
                emit_mid_load(1)
                for o in range(32):
                    if inject is not None:
                        inject(o)
                    pair = o // 2
                    if o % 2 == 0 and pair >= 1 and pair + 1 < 16:
                        emit_mid_load(pair + 1)
                    ps = pg[o % 4]
                    w = [(sMID[pair % 2], ev[("midd", layer, c, pair)])]
                    if layer == 0:
                        w.append((sDVE, v_H))
                    else:
                        w.append((sACT, ev[("gel", 0, c, 31)]))
                    if layer == 1 and c == 0 and 16 <= o <= 23:
                        w.append((sDVE, ev[("trv2", 1, o - 16)]))
                    if o >= 4:
                        w.append((sACT, ev[("gel", layer, c, o - 4)]))
                    elif layer == 0:
                        w.append((sACT, ev[("bsil", c, 30 + (o % 2))]))
                        w.append((sDVE, ev[("bh", c, 30 + (o % 2))]))
                    else:
                        w.append((sACT, ev[("gel", 0, c, 28 + o)]))
                    for hh in range(32):
                        for half in range(2):
                            pl.step("tensor",
                                    (lambda oo, h_, hf, ps_, pp: lambda g: g.matmul(
                                        ps_[:, 512 * hf:512 * hf + 512],
                                        mid_slot(pp)[:, 256 * h_ + 128 * (oo % 2):256 * h_ + 128 * (oo % 2) + 128],
                                        src_buf[:, FCH * h_ + 512 * hf:FCH * h_ + 512 * hf + 512],
                                        start=(h_ == 0), stop=(h_ == 31)))(o, hh, half, ps, pair),
                                    waits=(w if (hh == 0 and half == 0) else ()),
                                    inc=(sPE if (hh == 31 and half == 1) else None))
                    ev[("mpe", layer, c, o)] = sPE.n
                    if c == 0 and layer == 0:
                        pl.step("tensor", lambda g: g.nop(), inc=sPACE)
                    wge = [(sPE, ev[("mpe", layer, c, o)])]
                    if layer == 0 and o < 8 and c >= 1:
                        wge.append((sPE, ev[("ftr", c - 1, 7)]))
                    ev[("gel", layer, c, o)] = pl.step(
                        "scalar", (lambda oo, ps_: lambda g: g.activation(
                            dst_buf[:, FCH * oo:FCH * (oo + 1)], ps_[:],
                            AF.Gelu, bias=0.0,
                            scale=scale_sb[:, oo:oo + 1]))(o, ps),
                        waits=wge, inc=sACT)

            def d_inject(o):
                if 14 <= o <= 21:
                    s = o - 14
                    ap_affine(1, s)
                    ap_transpose(1, s, pg[s % 4][:, 0:512].bitcast(BF16),
                                 trw=((sACT, ev[("gel", 1, 0, 12 + s)]),))
                if 12 <= o <= 19:
                    s = o - 12
                    prev = (ev[("nrm2", 1, s - 2)] if s >= 2
                            else ev[("nrm2", 0, 4 + s)])
                    ap_load(1, s, prev)

            middle(0, h1b_d, Hbuf, G1buf, hs1_sb, sCVH1, g_ch1)
            middle(1, h2b_d, G1buf, Hbuf, hs2_sb, sCVH2, g_ch2,
                   inject=(d_inject if c == 0 else None))

            def emit_w3_load(pair):
                w = [(sCVA, g_cva)]
                if pair >= 2:
                    w.append((sPE, ev[("epe", c, 2 * (pair - 2) + 1)]))
                else:
                    w.append((sPE, ev[("mpe", 1, c, 2 * (pair + 14) + 1)]))
                for hh in range(32):
                    pl.step("sync", (lambda pp, h_: lambda g: g.dma_start(
                                mid_slot(pp)[:, 256 * h_:256 * (h_ + 1)],
                                w3b_d.ap()[128 * h_:128 * (h_ + 1),
                                           256 * pp:256 * (pp + 1)]))(pair, hh),
                            waits=(w if hh == 0 else ()),
                            inc=sMID[pair % 2], inc_k=16)
                ev[("w3d", c, pair)] = sMID[pair % 2].n

            emit_w3_load(0)
            emit_w3_load(1)
            for dd in range(8):
                pair = dd // 2
                if dd % 2 == 0 and pair >= 1 and pair + 1 < 4:
                    emit_w3_load(pair + 1)
                ps = pg[dd % 4]
                w = [(sMID[pair % 2], ev[("w3d", c, pair)]),
                     (sACT, ev[("gel", 1, c, 31)])]
                if dd >= 4:
                    w.append((sACT, ev[("eev", c, dd - 4)]))
                else:
                    w.append((sACT, ev[("gel", 1, c, 28 + dd)]))
                for hh in range(32):
                    for half in range(2):
                        pl.step("tensor",
                                (lambda d_, h_, hf, ps_, pp: lambda g: g.matmul(
                                    ps_[:, 512 * hf:512 * hf + 512],
                                    mid_slot(pp)[:, 256 * h_ + 128 * (d_ % 2):256 * h_ + 128 * (d_ % 2) + 128],
                                    Hbuf[:, FCH * h_ + 512 * hf:FCH * h_ + 512 * hf + 512],
                                    start=(h_ == 0), stop=(h_ == 31)))(dd, hh, half, ps, pair),
                                waits=(w if (hh == 0 and half == 0) else ()),
                                inc=(sPE if (hh == 31 and half == 1) else None))
                ev[("epe", c, dd)] = sPE.n
                w = [(sPE, ev[("epe", c, dd)])]
                if c >= 1 and dd == 0:
                    w.append((sPE, ev[("ftr", c - 1, 7)]))
                ev[("eev", c, dd)] = pl.step(
                    "scalar", (lambda d_, ps_: lambda g: g.activation(
                        YT[:, FCH * d_:FCH * (d_ + 1)], ps_[:], AF.Copy,
                        bias=0.0, scale=s3_sb[:, d_:d_ + 1]))(dd, ps),
                    waits=w, inc=sACT)

            for s in range(8):
                par = s % 2
                w = []
                if c == 0:
                    w.append((sACT, ev[("nrm2", 1, 6 + par)]))
                prev_x2 = ev.get(("x2", c, s - 2)) or ev.get(("x2", c - 1, s + 6))
                if prev_x2:
                    w.append((sDVE, prev_x2))
                d_x1b = pl.step("sync", (lambda r0, pp: lambda g: g.dma_start(
                            x1b[pp][:], x1_d.ap()[r0:r0 + 128, :]))(tc0 + 128 * s, par),
                        waits=w, inc=sX1[par], inc_k=16)
                prev_aff3 = ev.get(("aff3", c, s - 2))
                for i in range(8):
                    w = [(sACT, ev[("eev", c, 7)])]
                    if prev_aff3:
                        w.append((sACT, prev_aff3))
                    pl.step("tensor", (lambda pp, ii, s_: lambda g: g.transpose(
                                ytk[pp][:, 128 * ii:128 * ii + 128],
                                YT[:, FCH * ii + 128 * s_:FCH * ii + 128 * s_ + 128],
                                ident[:]))(par, i, s),
                            waits=w, inc=(sPE if i == 7 else None))
                ev[("ftr", c, s)] = sPE.n
                pl.step("vector", (lambda pp: lambda g: g.bn_stats(
                            stat[pp][:, 0:6], ytk[pp][:, 0:512]))(par),
                        waits=((sPE, ev[("ftr", c, s)]),), inc=sDVE)
                pl.step("vector", (lambda pp: lambda g: g.bn_stats(
                            stat[pp][:, 6:12], ytk[pp][:, 512:1024]))(par),
                        inc=sDVE)
                v_st3 = pl.step("vector", (lambda pp: lambda g: g.bn_aggr(
                            mv[pp][:], stat[pp][:]))(par), inc=sDVE)
                ev[("fst", c, s)] = v_st3
                a_l3 = pl.step("scalar", (lambda pp: lambda g: g.activation(
                            rstd[pp][:], mv[pp][:, 1:2], AF.Ln, bias=eps_c[:],
                            scale=1.0))(par),
                        waits=((sDVE, v_st3), (sGPC, g_csts)), inc=sACT)
                a_e3 = pl.step("scalar", (lambda pp: lambda g: g.activation(
                            rstd[pp][:], rstd[pp][:], AF.Exp, bias=0.0,
                            scale=-0.5))(par),
                        waits=((sACT, a_l3),), inc=sACT)
                v_n3 = pl.step("vector", (lambda pp: lambda g: g.scalar_tensor_tensor(
                            nmr[pp][:], mv[pp][:, 0:1], -1.0, rstd[pp][:],
                            ALU.mult, ALU.mult))(par),
                        waits=((sACT, a_e3),), inc=sDVE)
                w = [(sDVE, v_n3)]
                prev = ev.get(("x2", c, s - 1)) or ev.get(("x2", c - 1, 7))
                if prev:
                    w.append((sDVE, prev))
                if c == 0 and s == 0:
                    w.append((sPE, ev[("tr2", 1, 7)]))
                ev[("aff3", c, s)] = pl.step(
                    "scalar", (lambda pp: lambda g: g.activation(
                        ft3, ytk[pp][:], AF.Identity,
                        bias=nmr[pp][:], scale=rstd[pp][:]))(par),
                    waits=w, inc=sACT)
                w = [(sACT, ev[("aff3", c, s)]), (sX1[par], d_x1b)]
                if sOUT[par].n > 0:
                    w.append((sOUT[par], sOUT[par].n))
                ev[("x2", c, s)] = pl.step(
                    "vector", (lambda pp: lambda g: g.tensor_add(
                        x2b[pp][:], ft3, x1b[pp][:]))(par),
                    waits=w, inc=sDVE)
                pl.step("sync", (lambda r0, pp: lambda g: g.dma_start(
                            out_d.ap()[r0:r0 + 128, :], x2b[pp][:]))(tc0 + 128 * s, par),
                        waits=((sDVE, ev[("x2", c, s)]),), inc=sOUT[par], inc_k=16)

        es_f.close()
        final_waits = [(sOUT[0], sOUT[0].n), (sOUT[1], sOUT[1].n)]
        if taps:
            final_waits.append((sTAP, sTAP.n))
        pl.step("sync", None, waits=tuple(final_waits))

        pl.emit(block)
    es_sem.close()
    return nc


_BUILT = {}


def _get_program(taps=()):
    key = tuple(sorted(taps))
    if key not in _BUILT:
        _BUILT[key] = build_program(taps=key)
    return _BUILT[key]


def _prep_inputs(kw):
    x = np.asarray(kw["L0_attn_x"]).reshape(S, D)
    wq = np.asarray(kw["L0_attn_wq"])
    wk = np.asarray(kw["L0_attn_wk"])
    wv = np.asarray(kw["L0_attn_wv"])
    wo = np.asarray(kw["L0_attn_wo"])
    perm = np.concatenate([np.arange(0, HD, 2), np.arange(1, HD, 2)])
    shared = dict(
        wqT=np.ascontiguousarray(wq[perm].T),
        wkT=np.ascontiguousarray(wk[perm].T),
        wvT=np.ascontiguousarray(wv.T),
        woT=np.ascontiguousarray(wo.T),
        w1T=np.ascontiguousarray(np.asarray(kw["L0_ffn_w1_w"]).T),
        w2T=np.ascontiguousarray(np.asarray(kw["L0_ffn_w2_w"]).T),
        h1T=np.ascontiguousarray(np.asarray(kw["L0_ffn_h1_w"]).T),
        h2T=np.ascontiguousarray(np.asarray(kw["L0_ffn_h2_w"]).T),
        w3T=np.ascontiguousarray(np.asarray(kw["L0_ffn_w3_w"]).T),
        s1c=np.ascontiguousarray(np.asarray(kw["L0_ffn_w1_s"]).reshape(DH // 128, 128).T),
        s2c=np.ascontiguousarray(np.asarray(kw["L0_ffn_w2_s"]).reshape(DH // 128, 128).T),
        hs1c=np.ascontiguousarray(np.asarray(kw["L0_ffn_h1_s"]).reshape(DH // 128, 128).T),
        hs2c=np.ascontiguousarray(np.asarray(kw["L0_ffn_h2_s"]).reshape(DH // 128, 128).T),
        s3c=np.ascontiguousarray(np.asarray(kw["L0_ffn_w3_s"]).reshape(D // 128, 128).T),
    )
    in_maps = []
    for c in range(NCORES):
        m = dict(shared)
        m["x"] = np.ascontiguousarray(x[c * TLOC:(c + 1) * TLOC])
        m["coffm1"] = np.full((128, 1), float(c * TLOC - 1), dtype=np.float32)
        in_maps.append(m)
    return in_maps


def kernel(**kw):
    nc = _get_program(())
    in_maps = _prep_inputs(kw)
    res = run_bass_kernel_spmd(nc, in_maps, core_ids=list(range(NCORES)))
    out = np.concatenate([res.results[c]["out"] for c in range(NCORES)], axis=0)
    return out.reshape(1, S, D)
